# version 01
# baseline (speedup 1.0000x reference)
import contextlib

import numpy as np
import ml_dtypes

import concourse.bass as bass
import concourse.mybir as mybir
from concourse.bass_utils import run_bass_kernel_spmd

F32 = mybir.dt.float32
BF16 = mybir.dt.bfloat16
AF = mybir.ActivationFunctionType
ALU = mybir.AluOpType
AX = mybir.AxisListType

bfnp = ml_dtypes.bfloat16

B, S, D = 4, 2048, 2048
NH, KVH, DH = 16, 4, 128
THETA = 10000.0
DHS = 1.0 / np.sqrt(128.0)
FH = 4 * D
GQ = 64
E, EI = 32, 768
TOPK = 8
NL = 8
TOK = NL * 128
NDC = D // 128
LN_EPS = 1e-5
RMS_EPS = float(np.finfo(np.float32).eps)
NEG = -1e30
BIG = 1e30
HBND = 0.5009765625

ENGINES = ("sync", "scalar", "vector", "tensor", "gpsimd")


class Ch:
    def __init__(self, stack, nc, name):
        self.sem = stack.enter_context(nc.semaphore(name))
        self.n = 0
        self.name = name


class Prog:
    def __init__(self, nc):
        self.nc = nc
        self.q = {e: [] for e in ENGINES}
        self.stack = contextlib.ExitStack()

    def ch(self, name):
        return Ch(self.stack, self.nc, name)

    def wait(self, eng, ch, val=None):
        v = ch.n if val is None else val
        if v <= 0:
            return
        self.q[eng].append(lambda e, s=ch.sem, v=v: e.wait_ge(s, v))

    def emit(self, eng, fn, inc=None):
        tgt = None
        if inc is not None:
            ch, delta = inc
            ch.n += delta
            tgt = (ch.sem, delta)

        def run(e, fn=fn, tgt=tgt):
            instr = fn(e)
            if tgt is not None:
                instr.then_inc(tgt[0], tgt[1])

        self.q[eng].append(run)

    def finalize(self):
        nc = self.nc
        with nc.Block() as blk:
            for e in ENGINES:
                if self.q[e]:
                    getattr(blk, e)(lambda eng, e=e: [f(eng) for f in self.q[e]])
        self.stack.close()
        return nc


def ts(i, w=128):
    return slice(i * w, (i + 1) * w)


def nk_of(l):
    return min(2 * l + 3, 16)


TEST_XF = False
SKIP_QUANT = False
SKIP_CC = False
CC_DMA_BARRIER = False


def build(debug=False):
    nc = bass.Bass("TRN2", target_bir_lowering=False, debug=False)

    def din(name, shape):
        return nc.dram_tensor(name, list(shape), F32, kind="ExternalInput").ap()

    xo = din("xo", (TOK, D))
    xoT = din("xoT", (D, TOK))
    xaT = din("xaT", (D, S))
    wqT = din("wqT", (D, D))
    wkT = din("wkT", (D, KVH * DH))
    wvT = din("wvT", (D, KVH * DH))
    woT = din("woT", (D, D))
    gu_sh = din("gu_sh", (2 * FH // 8, D))
    wp_sh = din("wp_sh", (D // 8, FH))
    wqkv = din("wqkv", (D, D + 2 * KVH * DH))
    bqkv_c = din("bqkv_c", (128, 24))
    vb = din("vb", (1, KVH * DH))
    wout = din("wout", (D, D))
    boutr = din("boutr", (1, D))
    gwT = din("gwT", (D, E))
    mgT = din("mgT", (E, D, EI))
    muT = din("muT", (E, D, EI))
    mdT = din("mdT", (E, EI, D))
    cosq = din("cosq", (DH, TOK))
    sinq = din("sinq", (DH, TOK))
    cosk = din("cosk", (DH, S))
    sink = din("sink", (DH, S))
    mask0 = din("mask0", (NL, 128, 256))
    ident = din("ident", (128, 128))

    y_out = nc.dram_tensor("y", [TOK, D], F32, kind="ExternalOutput").ap()
    dbg = {}
    if debug:
        for nm in ("dx1", "dx2", "dx3"):
            dbg[nm] = nc.dram_tensor(nm, [TOK, D], F32, kind="ExternalOutput").ap()
    if debug == 2:
        dbg["dmw"] = nc.dram_tensor("dmw", [128, NL * E], F32, kind="ExternalOutput").ap()
        dbg["dhe"] = nc.dram_tensor("dhe", [6 * 128, TOK], BF16, kind="ExternalOutput").ap()
        dbg["dqp"] = nc.dram_tensor("dqp", [D, TOK], F32, kind="ExternalOutput").ap()
        dbg["dqf"] = nc.dram_tensor("dqf", [D, TOK], BF16, kind="ExternalOutput").ap()
        dbg["dkf"] = nc.dram_tensor("dkf", [KVH * DH, S], BF16, kind="ExternalOutput").ap()
        dbg["dvt"] = nc.dram_tensor("dvt", [S, KVH * DH], BF16, kind="ExternalOutput").ap()
        dbg["daf"] = nc.dram_tensor("daf", [D, TOK], BF16, kind="ExternalOutput").ap()

    def dint(name, shape, dt=BF16):
        return nc.dram_tensor(name, list(shape), dt, kind="Internal").ap()

    sb_bf = dint("sb_bf", (TOK, D))
    cc_x_in = dint("cc_x_in", (TOK, D))
    cc_x_out = dint("cc_x_out", (2 * TOK, D))
    ccT_gu = dint("ccT_gu", (D, 2 * FH // 8))
    shT_gu = dint("shT_gu", (8 * D, 2 * FH // 8))
    ccT_wp = dint("ccT_wp", (FH, D // 8))
    shT_wp = dint("shT_wp", (8 * FH, D // 8))
    sp1 = dint("sp1", (TOK, D), F32)
    sp2 = dint("sp2", (TOK, D), F32)

    SBUF_BASE = 16512
    SBUF_LIMIT = 229376
    cur = [SBUF_BASE]

    def sb(name, shape, dt, at=None):
        itemsz = 2 if dt == BF16 else 4
        nbytes = int(np.prod(shape[1:])) * itemsz
        if at is None:
            off = (cur[0] + 63) // 64 * 64
            cur[0] = off + nbytes
        else:
            off = at + SBUF_BASE
        assert off % 32 == 0 and off + nbytes <= SBUF_LIMIT, f"SBUF overflow: {name} {off}+{nbytes}"
        return nc.alloc_sbuf_tensor_at(name, list(shape), dt, offset=off).ap()

    id_f = sb("id_f", (128, 128), F32)
    id_b = sb("id_b", (128, 128), BF16)
    cosq_s = sb("cosq_s", (128, TOK), BF16)
    sinq_s = sb("sinq_s", (128, TOK), BF16)
    mask_s = sb("mask_s", (128, NL * 256), BF16)
    ones_r = sb("ones_r", (1, 512), BF16)
    onesc = sb("onesc", (128, 1), BF16)
    vb_s = sb("vb_s", (1, KVH * DH), BF16)
    bq_s = sb("bq_s", (128, 24), F32)
    mw_s = sb("mw_s", (128, NL * E), F32)
    gw_s = sb("gw_s", (128, NDC * E), BF16)
    sm_f = sb("sm_f", (128, 64), F32)
    inv8 = sb("inv8", (128, 16), F32)
    ssc = sb("ssc", (128, 16), F32)
    t8 = sb("t8", (128, 96), F32)
    silu_b = sb("silu_b", (128, 1024), F32)
    PERS_END = cur[0]
    A0 = (PERS_END + 63) // 64 * 64 - SBUF_BASE

    def reg(name, shape, dt, off):
        return sb(name, shape, dt, at=A0 + off)

    xf = [reg(f"xf{i}", (128, S), BF16, i * 4096) for i in range(16)]
    qf = [reg(f"qf{i}", (128, TOK), BF16, 65536 + i * 2048) for i in range(16)]
    kf = [reg(f"kf{i}", (128, S), BF16, 98304 + i * 4096) for i in range(4)]
    vt = [reg(f"vt{i}", (128, KVH * DH), BF16, 114688 + i * 1024) for i in range(16)]
    wst = reg("wst", (128, 4096), BF16, 131072)
    afull = [reg(f"af{i}", (128, TOK), BF16, 139264 + i * 2048) for i in range(16)]
    coskb = reg("coskb", (128, S), BF16, 172032)
    sinkb = reg("sinkb", (128, S), BF16, 176128)
    Pt = reg("Pt", (128, 2048), BF16, 172032)
    PTt = reg("PTt", (128, 1024), BF16, 176128)
    upst0 = reg("upst0", (128, 2048), BF16, 176128)
    qn_nat = reg("qn_nat", (128, 2048), BF16, 180224)
    qn_out = reg("qn_out", (128, 2048), BF16, 184320)
    qn_tr = reg("qn_tr", (128, 2048), BF16, 188416)
    qn_c = qn_tr
    qn_sc = reg("qn_sc", (128, 32), F32, 192512)
    qn_th = reg("qn_th", (128, 32), F32, 192640)
    qn_sum = reg("qn_sum", (128, 32), F32, 192768)
    ARENA_END = 192896
    assert A0 + ARENA_END <= SBUF_LIMIT, f"arena overflow {A0 + ARENA_END}"

    zall = [reg(f"zz{i}", (128, 2048), F32, i * 8192) for i in range(8)]
    wo_og = [reg(f"wog{i}", (128, 512), BF16, 65536 + i * 1024) for i in range(16)]
    bout_row = wst[0:1, 2048:4096]
    x1t = reg("x1t", (128, 2048), F32, 98304)
    zsq = reg("zsq", (128, 2048), BF16, 106496)
    rsc1 = reg("rsc1", (128, 1024), F32, 180224)
    rsc2 = reg("rsc2", (128, 1024), F32, 184320)

    hT = ([reg(f"h{i}", (128, 1024), BF16, i * 2048) for i in range(32)] +
          [reg(f"h{32+i}", (128, 1024), BF16, 139264 + i * 2048) for i in range(16)] +
          [reg(f"h{48+i}", (128, 1024), BF16, 114688 + i * 2048) for i in range(8)] +
          [reg(f"h{56+i}", (128, 1024), BF16, 180224 + i * 2048) for i in range(6)] +
          [reg(f"h{62+i}", (128, 1024), BF16, 172032 + i * 2048) for i in range(2)])
    x1fm = qf
    upst1 = reg("upst1", (128, 2048), BF16, 110592)

    x3fm = qf
    wgS = [reg(f"wgS{i}", (128, 2048), BF16, 98304 + i * 4096) for i in range(3)]
    wuS = [reg(f"wuS{i}", (128, 2048), BF16, 110592 + i * 4096) for i in range(3)]
    wdR = [reg(f"wdR{i}", (128, 3072), BF16, 122880 + i * 6144) for i in range(2)]
    hE = [reg(f"he{i}", (128, 1024), BF16, 135168 + i * 2048) for i in range(6)]
    msilu = [silu_b[:, 0:512], silu_b[:, 512:1024]]

    psA = nc.alloc_psum_tensor("psA", [128, 2048], F32).ap()
    psE = nc.alloc_psum_tensor("psE", [128, 2048], F32).ap()
    psPT = psE[:, 0:512].bitcast(BF16)
    psTR = psE[:, 1024:1536].bitcast(BF16)

    p = Prog(nc)
    gld = p.ch("gld")
    sld = p.ch("sld")
    tld = p.ch("tld")
    mm = p.ch("mm")
    ev = p.ch("ev")
    se = p.ch("se")
    cc = p.ch("cc")

    def gdma(dst, src):
        p.emit("gpsimd", lambda e, d=dst, s=src: e.dma_start(out=d, in_=s), inc=(gld, 16))
        return gld.n

    def sdma(dst, src):
        p.emit("sync", lambda e, d=dst, s=src: e.dma_start(out=d, in_=s), inc=(sld, 16))
        return sld.n

    def tdma(dst, src, transpose=False):
        p.emit("scalar", lambda e, d=dst, s=src, t=transpose: e.dma_start(out=d, in_=s, transpose=t),
               inc=(tld, 16))
        return tld.n

    sdma(id_f, ident)
    gdma(id_b, ident)
    gdma(cosq_s, cosq)
    gdma(sinq_s, sinq)
    gdma(mask_s.rearrange("p (l w) -> p l w", w=256), mask0.transpose([1, 0, 2]))
    gdma(vb_s, vb)
    sdma(bq_s, bqkv_c)
    for dc in range(NDC):
        gdma(gw_s[:, dc * E:(dc + 1) * E], gwT[ts(dc), :])
    CONST_GLD = gld.n
    CONST_SLD = sld.n
    p.emit("vector", lambda e: e.memset(ones_r, 1.0), inc=(ev, 1))
    p.emit("vector", lambda e: e.memset(onesc, 1.0), inc=(ev, 1))
    ONES_EV = ev.n

    def q_tile(nat_src, wr_dst):
        ldn = gdma(qn_nat, nat_src)
        p.wait("vector", gld, ldn)
        natv = qn_nat.rearrange("p (g k) -> p g k", k=GQ)
        p.emit("vector", lambda e: e.tensor_reduce(out=qn_sum, in_=natv, axis=AX.X,
                                                   op=ALU.add, apply_absolute_value=True), inc=(ev, 1))
        p.wait("vector", ev, ev.n)
        p.emit("vector", lambda e: e.tensor_scalar(out=qn_sc, in0=qn_sum, scalar1=1.0 / GQ,
                                                   scalar2=1e-8, op0=ALU.mult, op1=ALU.max), inc=(ev, 1))
        p.wait("vector", ev, ev.n)
        p.emit("vector", lambda e: e.tensor_copy(out=qn_th.bitcast(BF16)[:, 0:32], in_=qn_sc), inc=(ev, 1))
        p.wait("vector", ev, ev.n)
        p.emit("vector", lambda e: e.tensor_copy(out=qn_sc, in_=qn_th.bitcast(BF16)[:, 0:32]), inc=(ev, 1))
        p.wait("vector", ev, ev.n)
        p.emit("vector", lambda e: e.tensor_scalar_mul(out=qn_th, in0=qn_sc, scalar1=HBND), inc=(ev, 1))
        p.wait("vector", ev, ev.n)
        thb = qn_th.rearrange("p (g o) -> p g o", o=1).broadcast_to([128, 32, GQ])
        scb = qn_sc.rearrange("p (g o) -> p g o", o=1).broadcast_to([128, 32, GQ])
        cv = qn_c.rearrange("p (g k) -> p g k", k=GQ)
        ov = qn_out.rearrange("p (g k) -> p g k", k=GQ)
        p.emit("vector", lambda e: e.tensor_tensor(out=cv, in0=natv, in1=thb, op=ALU.is_gt), inc=(ev, 1))
        p.wait("vector", ev, ev.n)
        p.emit("vector", lambda e: e.tensor_tensor(out=ov, in0=cv, in1=scb, op=ALU.mult), inc=(ev, 1))
        p.wait("vector", ev, ev.n)
        p.emit("vector", lambda e: e.tensor_scalar_mul(out=qn_nat, in0=qn_nat, scalar1=-1.0), inc=(ev, 1))
        p.wait("vector", ev, ev.n)
        p.emit("vector", lambda e: e.tensor_tensor(out=cv, in0=natv, in1=thb, op=ALU.is_gt), inc=(ev, 1))
        p.wait("vector", ev, ev.n)
        p.emit("vector", lambda e: e.tensor_tensor(out=cv, in0=cv, in1=scb, op=ALU.mult), inc=(ev, 1))
        p.wait("vector", ev, ev.n)
        p.emit("vector", lambda e: e.tensor_sub(out=qn_out, in0=qn_out, in1=qn_c), inc=(ev, 1))
        ev_q = ev.n
        p.wait("tensor", ev, ev_q)
        p.wait("tensor", gld, CONST_GLD)
        for rnd in range(2):
            p.wait("tensor", ev, ev.n)
            for b in range(8):
                p.emit("tensor", lambda e, rnd=rnd, b=b: e.transpose(
                    psTR[:, ts(b)], qn_out[:, ts(rnd * 8 + b)], id_b), inc=(mm, 1))
            p.wait("vector", mm, mm.n)
            p.emit("vector", lambda e, rnd=rnd: e.tensor_copy(
                out=qn_tr[:, rnd * 1024:(rnd + 1) * 1024], in_=psTR), inc=(ev, 1))
        p.wait("sync", ev, ev.n)
        stn = sdma(wr_dst, qn_tr.rearrange("p (c f) -> p c f", f=128))
        return ev_q, stn

    qev, qst = 0, 0
    for i in range(16) if not SKIP_QUANT else []:
        p.wait("gpsimd", ev, ev.n)
        p.wait("gpsimd", sld, qst)
        dst = ccT_gu.rearrange("(c p) h -> p c h", p=128)[:, :, ts(i)]
        qev, qst = q_tile(gu_sh[ts(i), :], dst)
    for ri in range(2) if not SKIP_QUANT else []:
        for s_ in range(4):
            p.wait("gpsimd", ev, ev.n)
            p.wait("gpsimd", sld, qst)
            dst = ccT_wp[s_ * 2048:(s_ + 1) * 2048, ts(ri)].rearrange("(c p) o -> p c o", p=128)
            qev, qst = q_tile(wp_sh[ts(ri), s_ * 2048:(s_ + 1) * 2048], dst)
    QUANT_EV = qev
    QUANT_SE = se.n
    QUANT_MM = mm.n
    QUANT_ST = qst
    def emit_wq_gather():
        if SKIP_QUANT or SKIP_CC:
            return cc.n
        p.wait("gpsimd", sld, QUANT_ST)
        p.emit("gpsimd", lambda e: e.collective_compute(
            "AllGather", ALU.bypass, [list(range(8))], [ccT_gu.opt()], [shT_gu.opt()]), inc=(cc, 1))
        p.emit("gpsimd", lambda e: e.collective_compute(
            "AllGather", ALU.bypass, [list(range(8))], [ccT_wp.opt()], [shT_wp.opt()]), inc=(cc, 1))
        p.wait("gpsimd", cc, cc.n)
        p.wait("sync", cc, cc.n)
        p.wait("scalar", cc, cc.n)
        return cc.n

    state = {"CC_X": 0, "CC_W": 0}

    def ln_batch():
        z_ev = ev.n
        for tc in range(NL):
            p.wait("vector", ev, z_ev)
            p.emit("vector", lambda e, tc=tc: e.reduce_sum(
                out=sm_f[:, tc:tc + 1], in_=zall[tc], axis=AX.X), inc=(ev, 1))
            p.wait("scalar", ev, ev.n)
            p.emit("scalar", lambda e, tc=tc: e.activation(
                out=zsq, in_=zall[tc], func=AF.Square, bias=0.0, scale=1.0,
                accum_out=sm_f[:, 8 + tc:9 + tc]), inc=(se, 1))
        p.wait("vector", se, se.n)
        p.wait("vector", ev, ev.n)
        p.emit("vector", lambda e: e.tensor_scalar_mul(out=sm_f[:, 16:24], in0=sm_f[:, 0:8],
                                                       scalar1=1.0 / D), inc=(ev, 1))
        p.emit("vector", lambda e: e.tensor_scalar_mul(out=sm_f[:, 24:32], in0=sm_f[:, 8:16],
                                                       scalar1=1.0 / D), inc=(ev, 1))
        p.wait("vector", ev, ev.n)
        p.emit("vector", lambda e: e.tensor_mul(out=sm_f[:, 40:48], in0=sm_f[:, 16:24],
                                                in1=sm_f[:, 16:24]), inc=(ev, 1))
        p.wait("vector", ev, ev.n)
        p.emit("vector", lambda e: e.tensor_sub(out=sm_f[:, 48:56], in0=sm_f[:, 24:32],
                                                in1=sm_f[:, 40:48]), inc=(ev, 1))
        p.wait("vector", ev, ev.n)
        p.emit("vector", lambda e: e.tensor_scalar_add(out=sm_f[:, 48:56], in0=sm_f[:, 48:56],
                                                       scalar1=LN_EPS), inc=(ev, 1))
        p.wait("scalar", ev, ev.n)
        p.emit("scalar", lambda e: e.activation(out=sm_f[:, 56:64], in_=sm_f[:, 48:56],
                                                func=AF.Ln, bias=0.0, scale=1.0), inc=(se, 1))
        p.wait("scalar", se, se.n)
        p.emit("scalar", lambda e: e.activation(out=inv8[:, 0:8], in_=sm_f[:, 56:64],
                                                func=AF.Exp, bias=0.0, scale=-0.5), inc=(se, 1))
        p.wait("vector", se, se.n)
        p.emit("vector", lambda e: e.tensor_mul(out=inv8[:, 8:16], in0=sm_f[:, 16:24],
                                                in1=inv8[:, 0:8]), inc=(ev, 1))
        p.wait("vector", ev, ev.n)
        p.emit("vector", lambda e: e.tensor_scalar_mul(out=inv8[:, 8:16], in0=inv8[:, 8:16],
                                                       scalar1=-1.0), inc=(ev, 1))
        return ev.n

    def emit_attention(layer):
        L1 = layer == 1
        kvgrp = (lambda h: h % KVH) if L1 else (lambda h: h // (NH // KVH))

        if L1:
            p.wait("scalar", mm, mm.n)
            p.wait("scalar", ev, ev.n)
            p.wait("scalar", sld, sld.n)
            p.wait("gpsimd", mm, mm.n)
            p.wait("gpsimd", ev, ev.n)
            p.wait("gpsimd", sld, sld.n)

        xf_n = []
        for dc in range(NDC):
            if not L1:
                xf_n.append(("g", gdma(xf[dc], xaT[ts(dc), :])))
            else:
                p.wait("scalar", cc, state["CC_X"])
                for blk in range(4):
                    src = cc_x_out[blk * 512:(blk + 1) * 512, ts(dc)]
                    tdma(xf[dc][:, blk * 512:(blk + 1) * 512], src, transpose=True)
                xf_n.append(("t", tld.n))

        def wxf(eng, dc):
            k, n = xf_n[dc]
            p.wait(eng, gld if k == "g" else tld, n)

        xofm = afull
        xo_n = []
        for dc in range(NDC):
            if not L1:
                xo_n.append(("g", gdma(xofm[dc], xoT[ts(dc), :])))
            else:
                p.wait("scalar", cc, state["CC_X"])
                for blk in range(2):
                    src = cc_x_in[blk * 512:(blk + 1) * 512, ts(dc)]
                    tdma(xofm[dc][:, blk * 512:(blk + 1) * 512], src, transpose=True)
                xo_n.append(("t", tld.n))

        def wxo(eng, dc):
            k, n = xo_n[dc]
            p.wait(eng, gld if k == "g" else tld, n)

        if not L1:
            ckn = gdma(coskb, cosk)
            skn = gdma(sinkb, sink)

        wsrc_q = wqT if not L1 else wqkv[:, 0:D]
        wq_mark = [0, 0]
        for och in range(16):
            par = och % 2
            p.wait("gpsimd", mm, wq_mark[par])
            src = wsrc_q[:, ts(och)].rearrange("(c p) o -> p c o", p=128)
            wn = gdma(wst[:, par * 2048:par * 2048 + 2048].rearrange("p (c o) -> p c o", o=128), src)
            qp = psA[:, 0:1024] if par == 0 else psA[:, 1024:2048]
            p.wait("tensor", gld, wn)
            for dc in range(NDC):
                wxo("tensor", dc)
            p.wait("tensor", ev, ev.n)
            p.wait("tensor", se, se.n)
            for half in range(2):
                for dc in range(NDC):
                    p.emit("tensor", lambda e, dc=dc, half=half, par=par, qp=qp:
                           e.matmul(qp[:, half * 512:(half + 1) * 512],
                                    wst[:, par * 2048 + dc * 128: par * 2048 + (dc + 1) * 128],
                                    (xf if TEST_XF else xofm)[dc][:, half * 512:(half + 1) * 512],
                                    start=(dc == 0), stop=(dc == NDC - 1)),
                           inc=(mm, 1) if dc == NDC - 1 else None)
            wq_mark[par] = mm.n
            mmq = mm.n
            if debug == 2 and not L1:
                p.wait("vector", mm, mmq)
                p.wait("vector", sld, sld.n)
                p.emit("vector", lambda e, qp=qp: e.tensor_copy(
                    out=x1t[:, 0:1024], in_=qp), inc=(ev, 1))
                p.wait("sync", ev, ev.n)
                sdma(dbg["dqp"][ts(och), :], x1t[:, 0:1024])
            if L1:
                p.wait("vector", mm, mmq)
                p.emit("vector", lambda e, och=och, qp=qp: e.tensor_scalar(
                    out=qf[och], in0=qp, scalar1=bq_s[:, och:och + 1], scalar2=1.0 / 128.0,
                    op0=ALU.add, op1=ALU.mult), inc=(ev, 1))
            else:
                p.wait("vector", mm, max(mmq, QUANT_MM))
                p.wait("vector", se, QUANT_SE)
                p.wait("vector", sld, QUANT_ST)
                p.wait("vector", gld, CONST_GLD)
                p.wait("vector", ev, ev.n)
                ra = rsc1
                rb = rsc2
                p.emit("vector", lambda e, qp=qp, ra=ra: e.tensor_mul(out=ra, in0=qp, in1=cosq_s), inc=(ev, 1))
                p.emit("vector", lambda e, qp=qp, rb=rb: e.tensor_mul(
                    out=rb[0:64, :], in0=qp[64:128, :], in1=sinq_s[0:64, :]), inc=(ev, 1))
                p.emit("vector", lambda e, qp=qp, rb=rb: e.tensor_mul(
                    out=rb[64:128, :], in0=qp[0:64, :], in1=sinq_s[64:128, :]), inc=(ev, 1))
                p.wait("vector", ev, ev.n)
                p.emit("vector", lambda e, och=och, ra=ra, rb=rb: e.tensor_sub(
                    out=qf[och][0:64, :], in0=ra[0:64, :], in1=rb[0:64, :]), inc=(ev, 1))
                p.emit("vector", lambda e, och=och, ra=ra, rb=rb: e.tensor_add(
                    out=qf[och][64:128, :], in0=ra[64:128, :], in1=rb[64:128, :]), inc=(ev, 1))

        wsrc_k = wkT if not L1 else wqkv[:, D:D + KVH * DH]
        wk_mark = [mm.n, mm.n]
        for och in range(KVH):
            par = och % 2
            p.wait("gpsimd", mm, wk_mark[par])
            src = wsrc_k[:, ts(och)].rearrange("(c p) o -> p c o", p=128)
            wn = gdma(wst[:, par * 2048:par * 2048 + 2048].rearrange("p (c o) -> p c o", o=128), src)
            p.wait("tensor", gld, wn)
            for dc in range(NDC):
                wxf("tensor", dc)
            p.wait("tensor", ev, max(ev.n, QUANT_EV))
            p.wait("tensor", se, se.n)
            for quad in range(4):
                for dc in range(NDC):
                    p.emit("tensor", lambda e, dc=dc, quad=quad, par=par:
                           e.matmul(psA[:, ts(quad, 512)],
                                    wst[:, par * 2048 + dc * 128: par * 2048 + (dc + 1) * 128],
                                    xf[dc][:, ts(quad, 512)],
                                    start=(dc == 0), stop=(dc == NDC - 1)),
                           inc=(mm, 1) if dc == NDC - 1 else None)
            wk_mark[par] = mm.n
            mmk = mm.n
            if L1:
                p.wait("vector", mm, mmk)
                p.emit("vector", lambda e, och=och: e.tensor_scalar_add(
                    out=kf[och], in0=psA, scalar1=bq_s[:, 16 + och:17 + och]), inc=(ev, 1))
            else:
                p.wait("vector", mm, mmk)
                p.wait("vector", gld, max(ckn, skn))
                for hh in range(2):
                    cs = coskb[:, hh * 1024:(hh + 1) * 1024]
                    sn = sinkb[:, hh * 1024:(hh + 1) * 1024]
                    pp = psA[:, hh * 1024:(hh + 1) * 1024]
                    dst = kf[och][:, hh * 1024:(hh + 1) * 1024]
                    p.wait("vector", ev, ev.n)
                    p.emit("vector", lambda e, pp=pp, cs=cs: e.tensor_mul(out=rsc1, in0=pp, in1=cs), inc=(ev, 1))
                    p.emit("vector", lambda e, pp=pp, sn=sn: e.tensor_mul(
                        out=rsc2[0:64, :], in0=pp[64:128, :], in1=sn[0:64, :]), inc=(ev, 1))
                    p.emit("vector", lambda e, pp=pp, sn=sn: e.tensor_mul(
                        out=rsc2[64:128, :], in0=pp[0:64, :], in1=sn[64:128, :]), inc=(ev, 1))
                    p.wait("vector", ev, ev.n)
                    p.emit("vector", lambda e, dst=dst: e.tensor_sub(
                        out=dst[0:64, :], in0=rsc1[0:64, :], in1=rsc2[0:64, :]), inc=(ev, 1))
                    p.emit("vector", lambda e, dst=dst: e.tensor_add(
                        out=dst[64:128, :], in0=rsc1[64:128, :], in1=rsc2[64:128, :]), inc=(ev, 1))

        wsrc_v = wvT if not L1 else wqkv[:, D + KVH * DH:]
        vmark = mm.n
        for vh in range(2):
            p.wait("gpsimd", mm, vmark)
            src = wsrc_v[:, vh * 256:(vh + 1) * 256].rearrange("(c p) o -> p c o", p=128)
            wn = gdma(wst.rearrange("p (c o) -> p c o", o=256), src)
            p.wait("tensor", gld, wn)
            p.wait("tensor", ev, ev.n)
            p.wait("tensor", se, max(se.n, QUANT_SE))
            for g in range(16):
                slot = g % 4
                vp = psA[:, slot * 512: slot * 512 + 256]
                p.wait("tensor", ev, ev.n)
                for dc in range(NDC):
                    wxf("tensor", dc)
                for dc in range(NDC):
                    p.emit("tensor", lambda e, dc=dc, g=g, vp=vp:
                           e.matmul(vp, xf[dc][:, ts(g)], wst[:, dc * 256:(dc + 1) * 256],
                                    start=(dc == 0), stop=(dc == NDC - 1 and not L1)),
                           inc=(mm, 1) if (dc == NDC - 1 and not L1) else None)
                if L1:
                    p.wait("tensor", ev, ONES_EV)
                    p.wait("tensor", gld, CONST_GLD)
                    p.emit("tensor", lambda e, vh=vh, vp=vp:
                           e.matmul(vp, ones_r[0:1, 0:128], vb_s[0:1, vh * 256:(vh + 1) * 256],
                                    start=False, stop=True), inc=(mm, 1))
                p.wait("vector", mm, mm.n)
                p.emit("vector", lambda e, g=g, vh=vh, vp=vp: e.tensor_copy(
                    out=vt[g][:, vh * 256:(vh + 1) * 256], in_=vp), inc=(ev, 1))
            vmark = mm.n

        if not L1:
            state["CC_W"] = emit_wq_gather()

        QP_MM = mm.n
        QP_SE = se.n
        p.wait("vector", se, QP_SE)

        nmax = sm_f[:, 32:33]
        sums = sm_f[:, 33:34]
        rs = sm_f[:, 34:35]
        last_exp_se = se.n
        for l in range(NL):
            nk = nk_of(l) if not L1 else 16
            nw = nk * 128
            mb0 = min(2 * l + 1, 14)
            for h in range(NH):
                g = kvgrp(h)
                p.wait("tensor", se, last_exp_se)
                p.wait("tensor", ev, ev.n)
                for c0 in range(0, nk, 4):
                    cw = min(4, nk - c0) * 128
                    p.emit("tensor", lambda e, h=h, l=l, g=g, c0=c0, cw=cw:
                           e.matmul(psA[:, c0 * 128: c0 * 128 + cw], qf[h][:, ts(l)],
                                    kf[g][:, c0 * 128: c0 * 128 + cw], start=True, stop=True),
                           inc=(mm, 1))
                mms = mm.n
                p.wait("vector", mm, mms)
                if not L1:
                    p.emit("vector", lambda e, l=l, mb0=mb0: e.tensor_tensor(
                        out=psA[:, mb0 * 128: mb0 * 128 + 256],
                        in0=psA[:, mb0 * 128: mb0 * 128 + 256],
                        in1=mask_s[:, l * 256:(l + 1) * 256], op=ALU.add), inc=(ev, 1))
                    p.wait("vector", ev, ev.n)
                p.emit("vector", lambda e, nw=nw: e.reduce_max(
                    out=nmax, in_=psA[:, 0:nw], axis=AX.X, negate=True), inc=(ev, 1))
                p.wait("scalar", ev, ev.n)
                p.wait("scalar", mm, mm.n)
                p.emit("scalar", lambda e, nw=nw: e.activation(
                    out=Pt[:, 0:nw], in_=psA[:, 0:nw], func=AF.Exp,
                    bias=nmax, scale=1.0, accum_out=sums), inc=(se, 1))
                last_exp_se = se.n
                p.wait("vector", se, last_exp_se)
                p.emit("vector", lambda e: e.reciprocal(out=rs, in_=sums), inc=(ev, 1))
                p.wait("vector", ev, ev.n)
                p.emit("vector", lambda e, nw=nw: e.tensor_scalar_mul(
                    out=Pt[:, 0:nw], in0=Pt[:, 0:nw], scalar1=rs), inc=(ev, 1))
                evP = ev.n
                avp = psE[:, 1024:1536] if h % 2 == 0 else psE[:, 1536:2048]
                p.wait("tensor", ev, evP)
                for r0 in range(0, nk, 8):
                    rn = min(8, nk - r0)
                    p.wait("tensor", ev, ev.n)
                    for b in range(rn):
                        p.emit("tensor", lambda e, r0=r0, b=b: e.transpose(
                            psPT[:, ts(b)], Pt[:, ts(r0 + b)], id_b), inc=(mm, 1))
                    p.wait("vector", mm, mm.n)
                    p.emit("vector", lambda e, rn=rn: e.tensor_copy(
                        out=PTt[:, 0:rn * 128], in_=psPT[:, 0:rn * 128]), inc=(ev, 1))
                    p.wait("tensor", ev, ev.n)
                    for b in range(rn):
                        kb = r0 + b
                        p.emit("tensor", lambda e, kb=kb, b=b, g=g, avp=avp:
                               e.matmul(avp[:, 0:128], vt[kb][:, ts(g)], PTt[:, ts(b)],
                                        start=(kb == 0), stop=(kb == nk - 1)),
                               inc=(mm, 1) if kb == nk - 1 else None)
                p.wait("vector", mm, max(mm.n, QP_MM))
                p.emit("vector", lambda e, h=h, l=l, avp=avp: e.tensor_copy(
                    out=afull[h][:, ts(l)], in_=avp[:, 0:128]), inc=(ev, 1))
        ATT_SE = se.n
        ATT_MM = mm.n
        if debug == 2 and not L1:
            p.wait("sync", se, ATT_SE)
            p.wait("sync", ev, ev.n)
            for i_ in range(16):
                sdma(dbg["dqf"][ts(i_), :], qf[i_])
                sdma(dbg["daf"][ts(i_), :], afull[i_])
                sdma(dbg["dvt"][ts(i_), :], vt[i_])
            for i_ in range(4):
                sdma(dbg["dkf"][ts(i_), :], kf[i_])
            DUMP_ST = sld.n

        wsrc_o = woT if not L1 else wout
        res_src = xo if not L1 else sp2
        zi_n = []
        for tc in range(NL):
            p.wait("sync", mm, QP_MM)
            if L1:
                p.wait("sync", sld, state["SP2_ST"])
            zi_n.append(sdma(zall[tc], res_src[ts(tc), :]))
        if L1:
            bon = gdma(bout_row, boutr)
        wo_mark = 0
        for og in range(4):
            p.wait("gpsimd", mm, max(ATT_MM, wo_mark))
            if debug == 2 and not L1:
                p.wait("gpsimd", sld, DUMP_ST)
            src = wsrc_o[:, ts(og, 512)].rearrange("(c p) o -> c p o", p=128)
            won = 0
            for i in range(16):
                won = gdma(wo_og[i], src[i])
            p.wait("tensor", gld, won)
            p.wait("tensor", se, ATT_SE)
            for tc in range(NL):
                op_ps = psA[:, (tc % 4) * 512:(tc % 4) * 512 + 512]
                p.wait("tensor", ev, ev.n)
                for i in range(16):
                    p.emit("tensor", lambda e, i=i, tc=tc, op_ps=op_ps:
                           e.matmul(op_ps, afull[i][:, ts(tc)], wo_og[i],
                                    start=(i == 0), stop=(i == 15 and not L1)),
                           inc=(mm, 1) if (i == 15 and not L1) else None)
                if L1:
                    p.wait("tensor", ev, ONES_EV)
                    p.wait("tensor", gld, bon)
                    p.emit("tensor", lambda e, og=og, op_ps=op_ps:
                           e.matmul(op_ps, ones_r[0:1, 0:128], bout_row[0:1, ts(og, 512)],
                                    start=False, stop=True), inc=(mm, 1))
                p.wait("vector", mm, mm.n)
                p.wait("vector", sld, zi_n[tc])
                p.emit("vector", lambda e, tc=tc, og=og, op_ps=op_ps: e.tensor_add(
                    out=zall[tc][:, ts(og, 512)], in0=op_ps, in1=zall[tc][:, ts(og, 512)]),
                    inc=(ev, 1))
            wo_mark = mm.n

        LN_EV = ln_batch()
        st_bf = 0
        for tc in range(NL):
            p.wait("scalar", ev, LN_EV)
            p.wait("scalar", sld, sld.n)
            p.emit("scalar", lambda e, tc=tc: e.activation(
                out=zsq, in_=zall[tc], func=AF.Identity,
                bias=inv8[:, 8 + tc:9 + tc], scale=inv8[:, tc:tc + 1]), inc=(se, 1))
            p.wait("sync", se, se.n)
            st_bf = sdma(sb_bf[ts(tc), :], zsq)
            p.wait("vector", ev, LN_EV)
            p.wait("vector", se, se.n)
            p.emit("vector", lambda e, tc=tc: e.tensor_scalar(
                out=zall[tc], in0=zall[tc], scalar1=inv8[:, tc:tc + 1],
                scalar2=inv8[:, 8 + tc:9 + tc], op0=ALU.mult, op1=ALU.add), inc=(ev, 1))
            if layer == 0:
                p.wait("sync", ev, ev.n)
                sdma(sp1[ts(tc), :], zall[tc])
        state["Y_BF_ST"] = st_bf
        state["Y_EV"] = ev.n

    def emit_ffn():
        p.wait("scalar", mm, mm.n)
        p.wait("scalar", sld, state["Y_BF_ST"])
        x1_n = []
        for dc in range(NDC):
            for blk in range(2):
                src = sb_bf[blk * 512:(blk + 1) * 512, ts(dc)]
                tdma(x1fm[dc][:, blk * 512:(blk + 1) * 512], src, transpose=True)
            x1_n.append(tld.n)
        p.wait("sync", cc, state["CC_W"])

        p.emit("vector", lambda e: e.memset(ssc[:, 0:8], 0.0), inc=(ev, 1))
        p.wait("vector", ev, state["Y_EV"])
        p.wait("vector", mm, mm.n)
        p.wait("vector", se, se.n)
        SS_EV0 = ev.n

        g_mark = [mm.n, mm.n]
        last_h_ev = ev.n
        last_sq_se = se.n
        for j in range(64):
            par = j % 2
            rg, cg = j // 16, j % 16
            ru = 4 + j // 16
            gsrc = shT_gu[rg * 2048:(rg + 1) * 2048, ts(cg)].rearrange("(c p) h -> p c h", p=128)
            usrc = shT_gu[ru * 2048:(ru + 1) * 2048, ts(cg)].rearrange("(c p) h -> p c h", p=128)
            p.wait("sync", mm, g_mark[par])
            gdst = wst[:, par * 2048: par * 2048 + 2048]
            udst = upst0 if par == 0 else upst1
            gn = sdma(gdst.rearrange("p (c h) -> p c h", h=128), gsrc)
            un = sdma(udst.rearrange("p (c h) -> p c h", h=128), usrc)
            for dc in range(NDC):
                p.wait("tensor", tld, x1_n[dc])
            p.wait("tensor", sld, un)
            p.wait("tensor", ev, last_h_ev)
            p.wait("tensor", se, se.n)
            for half in range(2):
                for dc in range(NDC):
                    p.emit("tensor", lambda e, dc=dc, half=half, par=par:
                           e.matmul(psA[:, half * 512:(half + 1) * 512],
                                    wst[:, par * 2048 + dc * 128: par * 2048 + (dc + 1) * 128],
                                    x1fm[dc][:, half * 512:(half + 1) * 512],
                                    start=(dc == 0), stop=(dc == NDC - 1)),
                           inc=(mm, 1) if dc == NDC - 1 else None)
            for half in range(2):
                for dc in range(NDC):
                    p.emit("tensor", lambda e, dc=dc, half=half, par=par:
                           e.matmul(psA[:, 1024 + half * 512: 1024 + (half + 1) * 512],
                                    (upst0 if par == 0 else upst1)[:, dc * 128:(dc + 1) * 128],
                                    x1fm[dc][:, half * 512:(half + 1) * 512],
                                    start=(dc == 0), stop=(dc == NDC - 1)),
                           inc=(mm, 1) if dc == NDC - 1 else None)
            g_mark[par] = mm.n
            mmj = mm.n
            p.wait("scalar", mm, mmj)
            p.wait("scalar", ev, ev.n)
            p.emit("scalar", lambda e: e.activation(out=silu_b, in_=psA[:, 0:1024],
                                                    func=AF.Silu, bias=0.0, scale=1.0), inc=(se, 1))
            p.wait("vector", se, se.n)
            p.wait("vector", mm, mmj)
            p.emit("vector", lambda e, j=j: e.tensor_mul(out=hT[j], in0=psA[:, 1024:2048],
                                                         in1=silu_b), inc=(ev, 1))
            last_h_ev = ev.n
            p.wait("scalar", ev, last_h_ev)
            p.emit("scalar", lambda e, j=j: e.activation(out=zsq[:, 0:1024], in_=hT[j],
                                                         func=AF.Square, bias=0.0, scale=1.0), inc=(se, 1))
            last_sq_se = se.n
            p.wait("tensor", se, last_sq_se)
            p.wait("tensor", ev, ev.n)
            for tc in range(NL):
                p.emit("tensor", lambda e, tc=tc: e.matmul(
                    psE[:, tc:tc + 1], zsq[:, ts(tc)], onesc, start=True, stop=True),
                    inc=(mm, 1) if tc == NL - 1 else None)
            p.wait("vector", mm, mm.n)
            p.wait("vector", se, se.n)
            p.emit("vector", lambda e: e.tensor_add(out=ssc[:, 0:8], in0=ssc[:, 0:8],
                                                    in1=psE[:, 0:8]), inc=(ev, 1))
        p.wait("vector", ev, ev.n)
        p.emit("vector", lambda e: e.tensor_scalar(out=ssc[:, 8:16], in0=ssc[:, 0:8],
                                                   scalar1=1.0 / FH, scalar2=RMS_EPS,
                                                   op0=ALU.mult, op1=ALU.add), inc=(ev, 1))
        p.wait("scalar", ev, ev.n)
        p.emit("scalar", lambda e: e.activation(out=ssc[:, 8:16], in_=ssc[:, 8:16],
                                                func=AF.Ln, bias=0.0, scale=1.0), inc=(se, 1))
        p.wait("scalar", se, se.n)
        p.emit("scalar", lambda e: e.activation(out=ssc[:, 8:16], in_=ssc[:, 8:16],
                                                func=AF.Exp, bias=0.0, scale=-0.5), inc=(se, 1))
        RMS_SE = se.n

        w_mark = [mm.n, mm.n]
        y2_st = 0
        for pair in range(4):
            tca, tcb = 2 * pair, 2 * pair + 1
            p.wait("tensor", ev, ev.n)
            for hc in range(64):
                par = hc % 2
                p.wait("sync", mm, w_mark[par])
                src = shT_wp.rearrange("(r q p) o -> q p r o", q=64, p=128)[hc]
                dst = wst[:, par * 2048: par * 2048 + 2048].rearrange("p (r o) -> p r o", o=256)
                wn = sdma(dst, src)
                p.wait("tensor", sld, wn)
                for t_i, tc in ((0, tca), (1, tcb)):
                    pps = psA if t_i == 0 else psE
                    for og8 in range(4):
                        p.emit("tensor", lambda e, hc=hc, tc=tc, og8=og8, pps=pps, par=par:
                               e.matmul(pps[:, og8 * 512:(og8 + 1) * 512],
                                        hT[hc][:, ts(tc)],
                                        wst[:, par * 2048 + og8 * 512: par * 2048 + (og8 + 1) * 512],
                                        start=(hc == 0), stop=(hc == 63)),
                               inc=(mm, 1) if (og8 == 3 and t_i == 1) else None)
                w_mark[par] = mm.n
            DN_MM = mm.n
            for t_i, tc in ((0, tca), (1, tcb)):
                pps = psA if t_i == 0 else psE
                p.wait("sync", ev, ev.n)
                rn = sdma(x1t, sp1[ts(tc), :])
                p.wait("vector", mm, DN_MM)
                p.wait("vector", sld, rn)
                p.wait("vector", se, RMS_SE)
                p.emit("vector", lambda e, pps=pps, tc=tc: e.scalar_tensor_tensor(
                    out=x1t, in0=pps, scalar=ssc[:, 8 + tc:9 + tc], in1=x1t,
                    op0=ALU.mult, op1=ALU.add, accum_out=sm_f[:, 0:1]), inc=(ev, 1))
                zev = ev.n
                p.wait("scalar", ev, zev)
                p.wait("scalar", se, se.n)
                p.emit("scalar", lambda e: e.activation(
                    out=zsq, in_=x1t, func=AF.Square, bias=0.0, scale=1.0,
                    accum_out=sm_f[:, 1:2]), inc=(se, 1))
                p.wait("vector", se, se.n)
                p.emit("vector", lambda e: e.tensor_scalar_mul(out=sm_f[:, 2:3], in0=sm_f[:, 0:1],
                                                               scalar1=1.0 / D), inc=(ev, 1))
                p.wait("vector", ev, ev.n)
                p.emit("vector", lambda e: e.tensor_mul(out=sm_f[:, 3:4], in0=sm_f[:, 2:3],
                                                        in1=sm_f[:, 2:3]), inc=(ev, 1))
                p.wait("vector", ev, ev.n)
                p.emit("vector", lambda e: e.tensor_scalar(out=sm_f[:, 4:5], in0=sm_f[:, 1:2],
                                                           scalar1=1.0 / D, scalar2=None,
                                                           op0=ALU.mult), inc=(ev, 1))
                p.wait("vector", ev, ev.n)
                p.emit("vector", lambda e: e.tensor_sub(out=sm_f[:, 4:5], in0=sm_f[:, 4:5],
                                                        in1=sm_f[:, 3:4]), inc=(ev, 1))
                p.wait("vector", ev, ev.n)
                p.emit("vector", lambda e: e.tensor_scalar_add(out=sm_f[:, 4:5], in0=sm_f[:, 4:5],
                                                               scalar1=LN_EPS), inc=(ev, 1))
                p.wait("scalar", ev, ev.n)
                p.emit("scalar", lambda e: e.activation(out=sm_f[:, 5:6], in_=sm_f[:, 4:5],
                                                        func=AF.Ln, bias=0.0, scale=1.0), inc=(se, 1))
                p.wait("scalar", se, se.n)
                p.emit("scalar", lambda e: e.activation(out=sm_f[:, 6:7], in_=sm_f[:, 5:6],
                                                        func=AF.Exp, bias=0.0, scale=-0.5), inc=(se, 1))
                p.wait("vector", se, se.n)
                p.emit("vector", lambda e: e.tensor_mul(out=sm_f[:, 7:8], in0=sm_f[:, 2:3],
                                                        in1=sm_f[:, 6:7]), inc=(ev, 1))
                p.wait("vector", ev, ev.n)
                p.emit("vector", lambda e: e.tensor_scalar_mul(out=sm_f[:, 7:8], in0=sm_f[:, 7:8],
                                                               scalar1=-1.0), inc=(ev, 1))
                p.wait("scalar", ev, ev.n)
                p.emit("scalar", lambda e: e.activation(
                    out=zsq, in_=x1t, func=AF.Identity,
                    bias=sm_f[:, 7:8], scale=sm_f[:, 6:7]), inc=(se, 1))
                p.wait("sync", se, se.n)
                sdma(cc_x_in[ts(tc), :], zsq)
                p.wait("vector", se, se.n)
                p.emit("vector", lambda e: e.tensor_scalar(
                    out=x1t, in0=x1t, scalar1=sm_f[:, 6:7],
                    scalar2=sm_f[:, 7:8], op0=ALU.mult, op1=ALU.add), inc=(ev, 1))
                p.wait("sync", ev, ev.n)
                y2_st = sdma(sp2[ts(tc), :], x1t)
        state["SP2_ST"] = y2_st
        p.wait("gpsimd", sld, sld.n)
        p.emit("gpsimd", lambda e: e.collective_compute(
            "AllGather", ALU.bypass, [[0, 1], [2, 3], [4, 5], [6, 7]],
            [cc_x_in.opt()], [cc_x_out.opt()]), inc=(cc, 1))
        p.wait("gpsimd", cc, cc.n)
        p.wait("sync", cc, cc.n)
        p.wait("scalar", cc, cc.n)
        state["CC_X"] = cc.n

    def emit_moe():
        p.wait("scalar", mm, mm.n)
        p.wait("scalar", sld, state["Y_BF_ST"])
        x3_n = []
        for dc in range(NDC):
            for blk in range(2):
                src = sb_bf[blk * 512:(blk + 1) * 512, ts(dc)]
                tdma(x3fm[dc][:, blk * 512:(blk + 1) * 512], src, transpose=True)
            x3_n.append(tld.n)
        p.wait("vector", sld, sld.n)
        p.wait("gpsimd", ev, ev.n)
        p.wait("gpsimd", se, se.n)
        p.wait("gpsimd", sld, sld.n)
        p.wait("gpsimd", tld, tld.n)

        p.wait("tensor", gld, CONST_GLD)
        for tc in range(NL):
            for dc in range(NDC):
                p.wait("tensor", tld, x3_n[dc])
            p.wait("tensor", ev, ev.n)
            for dc in range(NDC):
                p.emit("tensor", lambda e, tc=tc, dc=dc: e.matmul(
                    psE[:, 0:32], x3fm[dc][:, ts(tc)], gw_s[:, dc * E:(dc + 1) * E],
                    start=(dc == 0), stop=(dc == NDC - 1)),
                    inc=(mm, 1) if dc == NDC - 1 else None)
            mml = mm.n
            lo, lt, eq = t8[:, 0:32], t8[:, 32:64], t8[:, 64:96]
            cur, nmx, tau, den = sm_f[:, 0:1], sm_f[:, 1:2], sm_f[:, 2:3], sm_f[:, 3:4]
            p.wait("vector", mm, mml)
            p.emit("vector", lambda e: e.tensor_copy(out=lo, in_=psE[:, 0:32]), inc=(ev, 1))
            p.wait("vector", ev, ev.n)
            p.emit("vector", lambda e: e.tensor_copy(out=lt, in_=lo), inc=(ev, 1))
            p.wait("vector", ev, ev.n)
            for it in range(TOPK):
                p.emit("vector", lambda e: e.reduce_max(out=cur, in_=lt, axis=AX.X), inc=(ev, 1))
                p.wait("vector", ev, ev.n)
                if it == 0:
                    p.emit("vector", lambda e: e.tensor_scalar_mul(out=nmx, in0=cur,
                                                                   scalar1=-1.0), inc=(ev, 1))
                    p.wait("vector", ev, ev.n)
                if it == TOPK - 1:
                    p.emit("vector", lambda e: e.tensor_copy(out=tau, in_=cur), inc=(ev, 1))
                    p.wait("vector", ev, ev.n)
                    break
                p.emit("vector", lambda e: e.tensor_scalar(out=eq, in0=lt, scalar1=cur,
                                                           scalar2=-BIG, op0=ALU.is_equal,
                                                           op1=ALU.mult), inc=(ev, 1))
                p.wait("vector", ev, ev.n)
                p.emit("vector", lambda e: e.tensor_add(out=lt, in0=lt, in1=eq), inc=(ev, 1))
                p.wait("vector", ev, ev.n)
            p.wait("scalar", ev, ev.n)
            p.wait("scalar", se, se.n)
            p.emit("scalar", lambda e: e.activation(out=lt, in_=lo, func=AF.Exp,
                                                    bias=nmx, scale=1.0), inc=(se, 1))
            p.wait("vector", se, se.n)
            p.emit("vector", lambda e: e.tensor_scalar(out=eq, in0=lo, scalar1=tau,
                                                       scalar2=None, op0=ALU.is_ge), inc=(ev, 1))
            p.wait("vector", ev, ev.n)
            p.emit("vector", lambda e: e.tensor_mul(out=lt, in0=lt, in1=eq), inc=(ev, 1))
            p.wait("vector", ev, ev.n)
            p.emit("vector", lambda e: e.reduce_sum(out=den, in_=lt, axis=AX.X), inc=(ev, 1))
            p.wait("vector", ev, ev.n)
            p.emit("vector", lambda e: e.reciprocal(out=den, in_=den), inc=(ev, 1))
            p.wait("vector", ev, ev.n)
            p.emit("vector", lambda e, tc=tc: e.tensor_scalar_mul(
                out=mw_s[:, tc * E:(tc + 1) * E], in0=lt, scalar1=den), inc=(ev, 1))
            p.wait("vector", ev, ev.n)

        if debug == 2:
            p.wait("sync", ev, ev.n)
            sdma(dbg["dmw"], mw_s)

        nslab = 6 * E
        slab_mark = [mm.n, mm.n, mm.n]

        def slab_load(e_i, ic):
            si = (e_i * 6 + ic) % 3
            p.wait("gpsimd", mm, slab_mark[si])
            gn = gdma(wgS[si].rearrange("p (c h) -> p c h", h=128),
                      mgT[e_i, :, ts(ic)].rearrange("(c p) i -> p c i", p=128))
            un = gdma(wuS[si].rearrange("p (c h) -> p c h", h=128),
                      muT[e_i, :, ts(ic)].rearrange("(c p) i -> p c i", p=128))
            return max(gn, un)

        slab_n = {}
        for ic in range(3):
            slab_n[(0, ic)] = slab_load(0, ic)
        wd_mark = [mm.n, mm.n]
        dn_last = {}
        for e_i in range(E):
            for ic in range(6):
                si = (e_i * 6 + ic) % 3
                tgp = 0
                p.wait("tensor", gld, slab_n[(e_i, ic)])
                for dc in range(NDC):
                    p.wait("tensor", tld, x3_n[dc])
                for tg in range(2):
                    slot = tg
                    gp = psA[:, slot * 512: slot * 512 + 512]
                    up = psA[:, 1024 + slot * 512: 1024 + slot * 512 + 512]
                    p.wait("tensor", se, se.n)
                    p.wait("tensor", ev, ev.n)
                    for dc in range(NDC):
                        p.emit("tensor", lambda e, dc=dc, si=si, tg=tg, gp=gp:
                               e.matmul(gp, wgS[si][:, dc * 128:(dc + 1) * 128],
                                        x3fm[dc][:, tg * 512:(tg + 1) * 512],
                                        start=(dc == 0), stop=(dc == NDC - 1)),
                               inc=(mm, 1) if dc == NDC - 1 else None)
                    for dc in range(NDC):
                        p.emit("tensor", lambda e, dc=dc, si=si, tg=tg, up=up:
                               e.matmul(up, wuS[si][:, dc * 128:(dc + 1) * 128],
                                        x3fm[dc][:, tg * 512:(tg + 1) * 512],
                                        start=(dc == 0), stop=(dc == NDC - 1)),
                               inc=(mm, 1) if dc == NDC - 1 else None)
                    mmg = mm.n
                    sis = msilu[tg]
                    p.wait("scalar", mm, mmg)
                    p.wait("scalar", ev, ev.n)
                    p.emit("scalar", lambda e, gp=gp, sis=sis: e.activation(
                        out=sis, in_=gp, func=AF.Silu, bias=0.0, scale=1.0), inc=(se, 1))
                    p.wait("vector", se, se.n)
                    p.wait("vector", mm, dn_last.get(e_i - 1, 0))
                    if debug == 2 and e_i == 1:
                        p.wait("vector", sld, sld.n)
                    p.emit("vector", lambda e, ic=ic, tg=tg, up=up, sis=sis: e.tensor_mul(
                        out=hE[ic][:, tg * 512:(tg + 1) * 512], in0=up, in1=sis), inc=(ev, 1))
                slab_mark[si] = mm.n
                if e_i * 6 + ic + 3 < nslab:
                    nxt = e_i * 6 + ic + 3
                    slab_n[(nxt // 6, nxt % 6)] = slab_load(nxt // 6, nxt % 6)
            H_EV = ev.n
            if debug == 2 and e_i == 0:
                p.wait("sync", ev, H_EV)
                for ic_ in range(6):
                    sdma(dbg["dhe"][ts(ic_), :], hE[ic_])
                DHE_ST = sld.n
            if debug == 2 and e_i == 1:
                pass
            for og in range(4):
                wpar = og % 2
                p.wait("gpsimd", mm, wd_mark[wpar])
                wn = gdma(wdR[wpar].rearrange("p (c o) -> p c o", o=512),
                          mdT[e_i].rearrange("(c p) o -> p c o", p=128)[:, :, ts(og, 512)])
                p.wait("tensor", gld, wn)
                p.wait("tensor", ev, H_EV)
                for tc in range(NL):
                    slot = tc % 4
                    dp = psE[:, slot * 512: slot * 512 + 512]
                    p.wait("tensor", ev, ev.n)
                    for ic in range(6):
                        p.emit("tensor", lambda e, ic=ic, tc=tc, wpar=wpar, dp=dp:
                               e.matmul(dp, hE[ic][:, ts(tc)],
                                        wdR[wpar][:, ic * 512:(ic + 1) * 512],
                                        start=(ic == 0), stop=(ic == 5)),
                               inc=(mm, 1) if ic == 5 else None)
                    p.wait("vector", mm, mm.n)
                    p.wait("vector", se, se.n)
                    p.emit("vector", lambda e, tc=tc, og=og, e_i=e_i, dp=dp: e.scalar_tensor_tensor(
                        out=zall[tc][:, ts(og, 512)], in0=dp,
                        scalar=mw_s[:, tc * E + e_i: tc * E + e_i + 1],
                        in1=zall[tc][:, ts(og, 512)], op0=ALU.mult, op1=ALU.add), inc=(ev, 1))
                wd_mark[wpar] = mm.n
            dn_last[e_i] = mm.n

        LN_EV = ln_batch()
        for tc in range(NL):
            p.wait("vector", ev, LN_EV)
            p.emit("vector", lambda e, tc=tc: e.tensor_scalar(
                out=zall[tc], in0=zall[tc], scalar1=inv8[:, tc:tc + 1],
                scalar2=inv8[:, 8 + tc:9 + tc], op0=ALU.mult, op1=ALU.add), inc=(ev, 1))
            p.wait("sync", ev, ev.n)
            sdma(y_out[ts(tc), :], zall[tc])

    emit_attention(0)
    if debug:
        p.wait("gpsimd", sld, sld.n)
        gdma(dbg["dx1"], sp1)
    emit_ffn()
    if debug:
        p.wait("gpsimd", sld, sld.n)
        gdma(dbg["dx2"], sp2)
    emit_attention(1)
    if debug:
        p.wait("sync", ev, ev.n)
        for tc in range(NL):
            sdma(dbg["dx3"][ts(tc), :], zall[tc])
    emit_moe()

    p.wait("sync", sld, sld.n)
    p.wait("sync", gld, gld.n)
    p.wait("sync", tld, tld.n)
    return p.finalize()


def _rope_tables():
    inv = (1.0 / THETA ** (np.arange(0, DH, 2, dtype=np.float32) / np.float32(DH))).astype(np.float32)
    pos = np.arange(S, dtype=np.float32)
    fr = pos[:, None] * inv[None, :]
    cos = np.repeat(np.cos(fr), 2, axis=-1).astype(np.float32)
    sin = np.repeat(np.sin(fr), 2, axis=-1).astype(np.float32)
    return cos, sin


def _mask_tiles(m):
    tri = np.triu(np.full((128, 128), NEG, dtype=np.float32), 1)
    zero = np.zeros((128, 128), dtype=np.float32)
    neg = np.full((128, 128), NEG, dtype=np.float32)
    out = np.zeros((NL, 128, 256), dtype=np.float32)
    for l in range(NL):
        G = 2 * l + m
        mb0 = min(2 * l + 1, 14)
        for w in range(2):
            b = mb0 + w
            t = zero if b <= G else (tri if b == G + 1 else neg)
            out[l, :, w * 128:(w + 1) * 128] = t
    return out


_CACHE = {}


def kernel(**inputs):
    if "nc" not in _CACHE:
        _CACHE["nc"] = build(debug=False)
    nc = _CACHE["nc"]

    x = np.ascontiguousarray(np.asarray(inputs["L0_attn_x"], dtype=np.float32))
    gu = np.ascontiguousarray(np.asarray(inputs["L0_ffn_w_gate_up"], dtype=np.float32))
    wp = np.ascontiguousarray(np.asarray(inputs["L0_ffn_w_proj"], dtype=np.float32))
    wqkv = np.ascontiguousarray(np.asarray(inputs["L1_attn_Wqkv"], dtype=np.float32))
    bqkv = np.asarray(inputs["L1_attn_bqkv"], dtype=np.float32)
    wout = np.ascontiguousarray(np.asarray(inputs["L1_attn_Wout"], dtype=np.float32))
    bout = np.asarray(inputs["L1_attn_bout"], dtype=np.float32)

    wqT = np.ascontiguousarray(np.asarray(inputs["L0_attn_Wq"], dtype=np.float32).T)
    wkT = np.ascontiguousarray(np.asarray(inputs["L0_attn_Wk"], dtype=np.float32).T)
    wvT = np.ascontiguousarray(np.asarray(inputs["L0_attn_Wv"], dtype=np.float32).T)
    woT = np.ascontiguousarray(np.asarray(inputs["L0_attn_Wo"], dtype=np.float32).T)
    gwT = np.ascontiguousarray(np.asarray(inputs["L1_ffn_gate_w"], dtype=np.float32).T)
    mgT = np.ascontiguousarray(np.asarray(inputs["L1_ffn_w_gate_proj"], dtype=np.float32).transpose(0, 2, 1))
    muT = np.ascontiguousarray(np.asarray(inputs["L1_ffn_w_up_proj"], dtype=np.float32).transpose(0, 2, 1))
    mdT = np.ascontiguousarray(np.asarray(inputs["L1_ffn_w_down_proj"], dtype=np.float32).transpose(0, 2, 1))
    bqkv_c = np.ascontiguousarray(bqkv.reshape(24, 128).T)
    vb = np.ascontiguousarray(bqkv[None, D + KVH * DH:])
    boutr = np.ascontiguousarray(bout[None, :])
    cos, sin = _rope_tables()
    coskT = np.ascontiguousarray(cos.T)
    sinkT = np.ascontiguousarray(sin.T)
    ident = np.eye(128, dtype=np.float32)

    in_maps = []
    for c in range(8):
        s, m = c // 2, c % 2
        own = np.concatenate([np.arange((2 * l + m) * 128, (2 * l + m + 1) * 128)
                              for l in range(NL)])
        xs = x[s]
        xsT = np.ascontiguousarray(xs.T)
        in_maps.append({
            "xo": np.ascontiguousarray(xs[own, :]),
            "xoT": np.ascontiguousarray(xsT[:, own]),
            "xaT": xsT,
            "wqT": wqT, "wkT": wkT, "wvT": wvT, "woT": woT,
            "gu_sh": np.ascontiguousarray(gu[c * (2 * FH // 8):(c + 1) * (2 * FH // 8), :]),
            "wp_sh": np.ascontiguousarray(wp[c * (D // 8):(c + 1) * (D // 8), :]),
            "wqkv": wqkv, "bqkv_c": bqkv_c, "vb": vb,
            "wout": wout, "boutr": boutr,
            "gwT": gwT, "mgT": mgT, "muT": muT, "mdT": mdT,
            "cosq": np.ascontiguousarray(cos[own, :].T * np.float32(DHS)),
            "sinq": np.ascontiguousarray(sin[own, :].T * np.float32(DHS)),
            "cosk": coskT, "sink": sinkT,
            "mask0": _mask_tiles(m), "ident": ident,
        })

    res = run_bass_kernel_spmd(nc, in_maps, core_ids=list(range(8)))
    out = np.empty((B, S, D), dtype=np.float32)
    for c in range(8):
        s, m = c // 2, c % 2
        y = np.asarray(res.results[c]["y"])
        for l in range(NL):
            out[s, (2 * l + m) * 128:(2 * l + m + 1) * 128, :] = y[ts(l), :]
    return out


# version 02
# speedup vs baseline: 1.2841x; 1.2841x over previous
import contextlib

import numpy as np
import ml_dtypes

import concourse.bass as bass
import concourse.mybir as mybir
from concourse.bass_utils import run_bass_kernel_spmd

F32 = mybir.dt.float32
BF16 = mybir.dt.bfloat16
AF = mybir.ActivationFunctionType
ALU = mybir.AluOpType
AX = mybir.AxisListType

bfnp = ml_dtypes.bfloat16

B, S, D = 4, 2048, 2048
NH, KVH, DH = 16, 4, 128
THETA = 10000.0
DHS = 1.0 / np.sqrt(128.0)
FH = 4 * D
GQ = 64
E, EI = 32, 768
TOPK = 8
NL = 8
TOK = NL * 128
NDC = D // 128
LN_EPS = 1e-5
RMS_EPS = float(np.finfo(np.float32).eps)
NEG = -1e30
BIG = 1e30
HBND = 0.5009765625

ENGINES = ("sync", "scalar", "vector", "tensor", "gpsimd")


class Ch:
    def __init__(self, stack, nc, name):
        self.sem = stack.enter_context(nc.semaphore(name))
        self.n = 0
        self.name = name


class Prog:
    def __init__(self, nc):
        self.nc = nc
        self.q = {e: [] for e in ENGINES}
        self.stack = contextlib.ExitStack()

    def ch(self, name):
        return Ch(self.stack, self.nc, name)

    def wait(self, eng, ch, val=None):
        v = ch.n if val is None else val
        if v <= 0:
            return
        self.q[eng].append(lambda e, s=ch.sem, v=v: e.wait_ge(s, v))

    def emit(self, eng, fn, inc=None):
        tgt = None
        if inc is not None:
            ch, delta = inc
            ch.n += delta
            tgt = (ch.sem, delta)

        def run(e, fn=fn, tgt=tgt):
            instr = fn(e)
            if tgt is not None:
                instr.then_inc(tgt[0], tgt[1])

        self.q[eng].append(run)

    def finalize(self):
        nc = self.nc
        with nc.Block() as blk:
            for e in ENGINES:
                if self.q[e]:
                    getattr(blk, e)(lambda eng, e=e: [f(eng) for f in self.q[e]])
        self.stack.close()
        return nc


def ts(i, w=128):
    return slice(i * w, (i + 1) * w)


def nk_of(l):
    return min(2 * l + 3, 16)


TEST_XF = False
SKIP_QUANT = False
SKIP_CC = False
CC_DMA_BARRIER = False


def build(debug=False):
    nc = bass.Bass("TRN2", target_bir_lowering=False, debug=False)

    def din(name, shape):
        return nc.dram_tensor(name, list(shape), F32, kind="ExternalInput").ap()

    xo = din("xo", (TOK, D))
    xoT = din("xoT", (D, TOK))
    xaT = din("xaT", (D, S))
    wqT = din("wqT", (D, D))
    wkT = din("wkT", (D, KVH * DH))
    wvT = din("wvT", (D, KVH * DH))
    woT = din("woT", (D, D))
    gu_sh = din("gu_sh", (2 * FH // 8, D))
    wp_sh = din("wp_sh", (D // 8, FH))
    wqkv = din("wqkv", (D, D + 2 * KVH * DH))
    bqkv_c = din("bqkv_c", (128, 24))
    vb = din("vb", (1, KVH * DH))
    wout = din("wout", (D, D))
    boutr = din("boutr", (1, D))
    gwT = din("gwT", (D, E))
    mgT = din("mgT", (E, D, EI))
    muT = din("muT", (E, D, EI))
    mdT = din("mdT", (E, EI, D))
    cosq = din("cosq", (DH, TOK))
    sinq = din("sinq", (DH, TOK))
    cosk = din("cosk", (DH, S))
    sink = din("sink", (DH, S))
    mask0 = din("mask0", (NL, 128, 256))
    ident = din("ident", (128, 128))

    y_out = nc.dram_tensor("y", [TOK, D], F32, kind="ExternalOutput").ap()
    dbg = {}
    if debug:
        for nm in ("dx1", "dx2", "dx3"):
            dbg[nm] = nc.dram_tensor(nm, [TOK, D], F32, kind="ExternalOutput").ap()
    if debug == 2:
        dbg["dmw"] = nc.dram_tensor("dmw", [128, NL * E], F32, kind="ExternalOutput").ap()
        dbg["dhe"] = nc.dram_tensor("dhe", [6 * 128, TOK], BF16, kind="ExternalOutput").ap()
        dbg["dqp"] = nc.dram_tensor("dqp", [D, TOK], F32, kind="ExternalOutput").ap()
        dbg["dqf"] = nc.dram_tensor("dqf", [D, TOK], BF16, kind="ExternalOutput").ap()
        dbg["dkf"] = nc.dram_tensor("dkf", [KVH * DH, S], BF16, kind="ExternalOutput").ap()
        dbg["dvt"] = nc.dram_tensor("dvt", [S, KVH * DH], BF16, kind="ExternalOutput").ap()
        dbg["daf"] = nc.dram_tensor("daf", [D, TOK], BF16, kind="ExternalOutput").ap()

    def dint(name, shape, dt=BF16):
        return nc.dram_tensor(name, list(shape), dt, kind="Internal").ap()

    sb_bf = dint("sb_bf", (TOK, D))
    cc_x_in = dint("cc_x_in", (TOK, D))
    cc_x_out = dint("cc_x_out", (2 * TOK, D))
    ccT_gu = dint("ccT_gu", (D, 2 * FH // 8))
    shT_gu = dint("shT_gu", (8 * D, 2 * FH // 8))
    ccT_wp = dint("ccT_wp", (FH, D // 8))
    shT_wp = dint("shT_wp", (8 * FH, D // 8))
    sp1 = dint("sp1", (TOK, D), F32)
    sp2 = dint("sp2", (TOK, D), F32)

    SBUF_BASE = 16512
    SBUF_LIMIT = 229376
    cur = [SBUF_BASE]

    def sb(name, shape, dt, at=None):
        itemsz = 2 if dt == BF16 else 4
        nbytes = int(np.prod(shape[1:])) * itemsz
        if at is None:
            off = (cur[0] + 63) // 64 * 64
            cur[0] = off + nbytes
        else:
            off = at + SBUF_BASE
        assert off % 32 == 0 and off + nbytes <= SBUF_LIMIT, f"SBUF overflow: {name} {off}+{nbytes}"
        return nc.alloc_sbuf_tensor_at(name, list(shape), dt, offset=off).ap()

    id_f = sb("id_f", (128, 128), F32)
    id_b = sb("id_b", (128, 128), BF16)
    cosq_s = sb("cosq_s", (128, TOK), BF16)
    sinq_s = sb("sinq_s", (128, TOK), BF16)
    mask_s = sb("mask_s", (128, NL * 256), BF16)
    ones_r = sb("ones_r", (1, 512), BF16)
    onesc = sb("onesc", (128, 1), BF16)
    vb_s = sb("vb_s", (1, KVH * DH), BF16)
    bq_s = sb("bq_s", (128, 24), F32)
    mw_s = sb("mw_s", (128, NL * E), F32)
    gw_s = sb("gw_s", (128, NDC * E), BF16)
    sm_f = sb("sm_f", (128, 64), F32)
    inv8 = sb("inv8", (128, 16), F32)
    ssc = sb("ssc", (128, 16), F32)
    t8 = sb("t8", (128, 96), F32)
    silu_b = sb("silu_b", (128, 1024), F32)
    PERS_END = cur[0]
    A0 = (PERS_END + 63) // 64 * 64 - SBUF_BASE

    def reg(name, shape, dt, off):
        return sb(name, shape, dt, at=A0 + off)

    xf = [reg(f"xf{i}", (128, S), BF16, i * 4096) for i in range(16)]
    qf = [reg(f"qf{i}", (128, TOK), BF16, 65536 + i * 2048) for i in range(16)]
    kf = [reg(f"kf{i}", (128, S), BF16, 98304 + i * 4096) for i in range(4)]
    vt = [reg(f"vt{i}", (128, KVH * DH), BF16, 114688 + i * 1024) for i in range(16)]
    wst = reg("wst", (128, 4096), BF16, 131072)
    afull = [reg(f"af{i}", (128, TOK), BF16, 139264 + i * 2048) for i in range(16)]
    coskb = reg("coskb", (128, S), BF16, 172032)
    sinkb = reg("sinkb", (128, S), BF16, 176128)
    Pt = reg("Pt", (128, 2048), BF16, 172032)
    PTt = reg("PTt", (128, 1024), BF16, 176128)
    upst0 = reg("upst0", (128, 2048), BF16, 176128)
    qn_nat = reg("qn_nat", (128, 2048), BF16, 180224)
    qn_out = reg("qn_out", (128, 2048), BF16, 184320)
    qn_tr = reg("qn_tr", (128, 2048), BF16, 188416)
    qn_c = qn_tr
    qn_sc = reg("qn_sc", (128, 32), F32, 192512)
    qn_th = reg("qn_th", (128, 32), F32, 192640)
    qn_sum = reg("qn_sum", (128, 32), F32, 192768)
    ARENA_END = 192896
    assert A0 + ARENA_END <= SBUF_LIMIT, f"arena overflow {A0 + ARENA_END}"

    zall = [reg(f"zz{i}", (128, 2048), F32, i * 8192) for i in range(8)]
    wo_og = [reg(f"wog{i}", (128, 512), BF16, 65536 + i * 1024) for i in range(16)]
    bout_row = wst[0:1, 2048:4096]
    x1t = reg("x1t", (128, 2048), F32, 98304)
    zsq = reg("zsq", (128, 2048), BF16, 106496)
    rsc1 = reg("rsc1", (128, 1024), F32, 180224)
    rsc2 = reg("rsc2", (128, 1024), F32, 184320)

    hT = ([reg(f"h{i}", (128, 1024), BF16, i * 2048) for i in range(32)] +
          [reg(f"h{32+i}", (128, 1024), BF16, 139264 + i * 2048) for i in range(16)] +
          [reg(f"h{48+i}", (128, 1024), BF16, 114688 + i * 2048) for i in range(8)] +
          [reg(f"h{56+i}", (128, 1024), BF16, 180224 + i * 2048) for i in range(6)] +
          [reg(f"h{62+i}", (128, 1024), BF16, 172032 + i * 2048) for i in range(2)])
    x1fm = qf
    upst1 = reg("upst1", (128, 2048), BF16, 110592)

    x3fm = qf
    wgS = [reg(f"wgS{i}", (128, 2048), BF16, 98304 + i * 4096) for i in range(3)]
    wuS = [reg(f"wuS{i}", (128, 2048), BF16, 110592 + i * 4096) for i in range(3)]
    wdR = [reg(f"wdR{i}", (128, 3072), BF16, 122880 + i * 6144) for i in range(2)]
    hE = [reg(f"he{i}", (128, 1024), BF16, 135168 + i * 2048) for i in range(6)]
    msilu = [silu_b[:, 0:512], silu_b[:, 512:1024]]

    psA = nc.alloc_psum_tensor("psA", [128, 2048], F32).ap()
    psE = nc.alloc_psum_tensor("psE", [128, 2048], F32).ap()
    psPT = psE[:, 0:512].bitcast(BF16)
    psTR = psE[:, 1024:1536].bitcast(BF16)

    p = Prog(nc)
    gld = p.ch("gld")
    sld = p.ch("sld")
    tld = p.ch("tld")
    mm = p.ch("mm")
    ev = p.ch("ev")
    se = p.ch("se")
    cc = p.ch("cc")

    def gdma(dst, src):
        p.emit("gpsimd", lambda e, d=dst, s=src: e.dma_start(out=d, in_=s), inc=(gld, 16))
        return gld.n

    def sdma(dst, src):
        p.emit("sync", lambda e, d=dst, s=src: e.dma_start(out=d, in_=s), inc=(sld, 16))
        return sld.n

    def tdma(dst, src, transpose=False):
        p.emit("scalar", lambda e, d=dst, s=src, t=transpose: e.dma_start(out=d, in_=s, transpose=t),
               inc=(tld, 16))
        return tld.n

    sdma(id_f, ident)
    gdma(id_b, ident)
    gdma(cosq_s, cosq)
    gdma(sinq_s, sinq)
    gdma(mask_s.rearrange("p (l w) -> p l w", w=256), mask0.transpose([1, 0, 2]))
    gdma(vb_s, vb)
    sdma(bq_s, bqkv_c)
    for dc in range(NDC):
        gdma(gw_s[:, dc * E:(dc + 1) * E], gwT[ts(dc), :])
    CONST_GLD = gld.n
    CONST_SLD = sld.n
    p.emit("vector", lambda e: e.memset(ones_r, 1.0), inc=(ev, 1))
    p.emit("vector", lambda e: e.memset(onesc, 1.0), inc=(ev, 1))
    ONES_EV = ev.n

    def q_tile(nat_src, wr_dst):
        ldn = gdma(qn_nat, nat_src)
        p.wait("vector", gld, ldn)
        natv = qn_nat.rearrange("p (g k) -> p g k", k=GQ)
        p.emit("vector", lambda e: e.tensor_reduce(out=qn_sum, in_=natv, axis=AX.X,
                                                   op=ALU.add, apply_absolute_value=True), inc=(ev, 1))
        p.wait("vector", ev, ev.n)
        p.emit("vector", lambda e: e.tensor_scalar(out=qn_sc, in0=qn_sum, scalar1=1.0 / GQ,
                                                   scalar2=1e-8, op0=ALU.mult, op1=ALU.max), inc=(ev, 1))
        p.wait("vector", ev, ev.n)
        p.emit("vector", lambda e: e.tensor_copy(out=qn_th.bitcast(BF16)[:, 0:32], in_=qn_sc), inc=(ev, 1))
        p.wait("vector", ev, ev.n)
        p.emit("vector", lambda e: e.tensor_copy(out=qn_sc, in_=qn_th.bitcast(BF16)[:, 0:32]), inc=(ev, 1))
        p.wait("vector", ev, ev.n)
        p.emit("vector", lambda e: e.tensor_scalar_mul(out=qn_th, in0=qn_sc, scalar1=HBND), inc=(ev, 1))
        p.wait("vector", ev, ev.n)
        thb = qn_th.rearrange("p (g o) -> p g o", o=1).broadcast_to([128, 32, GQ])
        scb = qn_sc.rearrange("p (g o) -> p g o", o=1).broadcast_to([128, 32, GQ])
        cv = qn_c.rearrange("p (g k) -> p g k", k=GQ)
        ov = qn_out.rearrange("p (g k) -> p g k", k=GQ)
        p.emit("vector", lambda e: e.tensor_tensor(out=cv, in0=natv, in1=thb, op=ALU.is_gt), inc=(ev, 1))
        p.wait("vector", ev, ev.n)
        p.emit("vector", lambda e: e.tensor_tensor(out=ov, in0=cv, in1=scb, op=ALU.mult), inc=(ev, 1))
        p.wait("vector", ev, ev.n)
        p.emit("vector", lambda e: e.tensor_scalar_mul(out=qn_nat, in0=qn_nat, scalar1=-1.0), inc=(ev, 1))
        p.wait("vector", ev, ev.n)
        p.emit("vector", lambda e: e.tensor_tensor(out=cv, in0=natv, in1=thb, op=ALU.is_gt), inc=(ev, 1))
        p.wait("vector", ev, ev.n)
        p.emit("vector", lambda e: e.tensor_tensor(out=cv, in0=cv, in1=scb, op=ALU.mult), inc=(ev, 1))
        p.wait("vector", ev, ev.n)
        p.emit("vector", lambda e: e.tensor_sub(out=qn_out, in0=qn_out, in1=qn_c), inc=(ev, 1))
        ev_q = ev.n
        p.wait("tensor", ev, ev_q)
        p.wait("tensor", gld, CONST_GLD)
        for rnd in range(2):
            p.wait("tensor", ev, ev.n)
            for b in range(8):
                p.emit("tensor", lambda e, rnd=rnd, b=b: e.transpose(
                    psTR[:, ts(b)], qn_out[:, ts(rnd * 8 + b)], id_b), inc=(mm, 1))
            p.wait("vector", mm, mm.n)
            p.emit("vector", lambda e, rnd=rnd: e.tensor_copy(
                out=qn_tr[:, rnd * 1024:(rnd + 1) * 1024], in_=psTR), inc=(ev, 1))
        p.wait("sync", ev, ev.n)
        stn = sdma(wr_dst, qn_tr.rearrange("p (c f) -> p c f", f=128))
        return ev_q, stn

    qev, qst = 0, 0
    for i in range(16) if not SKIP_QUANT else []:
        p.wait("gpsimd", ev, ev.n)
        p.wait("gpsimd", sld, qst)
        dst = ccT_gu.rearrange("(c p) h -> p c h", p=128)[:, :, ts(i)]
        qev, qst = q_tile(gu_sh[ts(i), :], dst)
    for ri in range(2) if not SKIP_QUANT else []:
        for s_ in range(4):
            p.wait("gpsimd", ev, ev.n)
            p.wait("gpsimd", sld, qst)
            dst = ccT_wp[s_ * 2048:(s_ + 1) * 2048, ts(ri)].rearrange("(c p) o -> p c o", p=128)
            qev, qst = q_tile(wp_sh[ts(ri), s_ * 2048:(s_ + 1) * 2048], dst)
    QUANT_EV = qev
    QUANT_SE = se.n
    QUANT_MM = mm.n
    QUANT_ST = qst
    def emit_wq_gather():
        if SKIP_QUANT or SKIP_CC:
            return cc.n
        p.wait("gpsimd", sld, QUANT_ST)
        p.emit("gpsimd", lambda e: e.collective_compute(
            "AllGather", ALU.bypass, [list(range(8))], [ccT_gu.opt()], [shT_gu.opt()]), inc=(cc, 1))
        p.emit("gpsimd", lambda e: e.collective_compute(
            "AllGather", ALU.bypass, [list(range(8))], [ccT_wp.opt()], [shT_wp.opt()]), inc=(cc, 1))
        p.wait("gpsimd", cc, cc.n)
        p.wait("sync", cc, cc.n)
        p.wait("scalar", cc, cc.n)
        return cc.n

    state = {"CC_X": 0, "CC_W": 0}

    def ln_batch():
        z_ev = ev.n
        for tc in range(NL):
            p.wait("vector", ev, z_ev)
            p.emit("vector", lambda e, tc=tc: e.reduce_sum(
                out=sm_f[:, tc:tc + 1], in_=zall[tc], axis=AX.X), inc=(ev, 1))
            p.wait("scalar", ev, ev.n)
            p.emit("scalar", lambda e, tc=tc: e.activation(
                out=zsq, in_=zall[tc], func=AF.Square, bias=0.0, scale=1.0,
                accum_out=sm_f[:, 8 + tc:9 + tc]), inc=(se, 1))
        p.wait("vector", se, se.n)
        p.wait("vector", ev, ev.n)
        p.emit("vector", lambda e: e.tensor_scalar_mul(out=sm_f[:, 16:24], in0=sm_f[:, 0:8],
                                                       scalar1=1.0 / D), inc=(ev, 1))
        p.emit("vector", lambda e: e.tensor_scalar_mul(out=sm_f[:, 24:32], in0=sm_f[:, 8:16],
                                                       scalar1=1.0 / D), inc=(ev, 1))
        p.wait("vector", ev, ev.n)
        p.emit("vector", lambda e: e.tensor_mul(out=sm_f[:, 40:48], in0=sm_f[:, 16:24],
                                                in1=sm_f[:, 16:24]), inc=(ev, 1))
        p.wait("vector", ev, ev.n)
        p.emit("vector", lambda e: e.tensor_sub(out=sm_f[:, 48:56], in0=sm_f[:, 24:32],
                                                in1=sm_f[:, 40:48]), inc=(ev, 1))
        p.wait("vector", ev, ev.n)
        p.emit("vector", lambda e: e.tensor_scalar_add(out=sm_f[:, 48:56], in0=sm_f[:, 48:56],
                                                       scalar1=LN_EPS), inc=(ev, 1))
        p.wait("scalar", ev, ev.n)
        p.emit("scalar", lambda e: e.activation(out=sm_f[:, 56:64], in_=sm_f[:, 48:56],
                                                func=AF.Ln, bias=0.0, scale=1.0), inc=(se, 1))
        p.wait("scalar", se, se.n)
        p.emit("scalar", lambda e: e.activation(out=inv8[:, 0:8], in_=sm_f[:, 56:64],
                                                func=AF.Exp, bias=0.0, scale=-0.5), inc=(se, 1))
        p.wait("vector", se, se.n)
        p.emit("vector", lambda e: e.tensor_mul(out=inv8[:, 8:16], in0=sm_f[:, 16:24],
                                                in1=inv8[:, 0:8]), inc=(ev, 1))
        p.wait("vector", ev, ev.n)
        p.emit("vector", lambda e: e.tensor_scalar_mul(out=inv8[:, 8:16], in0=inv8[:, 8:16],
                                                       scalar1=-1.0), inc=(ev, 1))
        return ev.n

    def emit_attention(layer):
        L1 = layer == 1
        kvgrp = (lambda h: h % KVH) if L1 else (lambda h: h // (NH // KVH))

        if L1:
            p.wait("scalar", mm, mm.n)
            p.wait("scalar", ev, ev.n)
            p.wait("scalar", sld, sld.n)
            p.wait("gpsimd", mm, mm.n)
            p.wait("gpsimd", ev, ev.n)
            p.wait("gpsimd", sld, sld.n)

        xf_n = []
        for dc in range(NDC):
            if not L1:
                xf_n.append(("g", gdma(xf[dc], xaT[ts(dc), :])))
            else:
                p.wait("scalar", cc, state["CC_X"])
                for blk in range(4):
                    src = cc_x_out[blk * 512:(blk + 1) * 512, ts(dc)]
                    tdma(xf[dc][:, blk * 512:(blk + 1) * 512], src, transpose=True)
                xf_n.append(("t", tld.n))

        def wxf(eng, dc):
            k, n = xf_n[dc]
            p.wait(eng, gld if k == "g" else tld, n)

        xofm = afull
        xo_n = []
        for dc in range(NDC):
            if not L1:
                xo_n.append(("g", gdma(xofm[dc], xoT[ts(dc), :])))
            else:
                p.wait("scalar", cc, state["CC_X"])
                for blk in range(2):
                    src = cc_x_in[blk * 512:(blk + 1) * 512, ts(dc)]
                    tdma(xofm[dc][:, blk * 512:(blk + 1) * 512], src, transpose=True)
                xo_n.append(("t", tld.n))

        def wxo(eng, dc):
            k, n = xo_n[dc]
            p.wait(eng, gld if k == "g" else tld, n)

        if not L1:
            ckn = gdma(coskb, cosk)
            skn = gdma(sinkb, sink)

        wsrc_q = wqT if not L1 else wqkv[:, 0:D]
        wq_mark = [0, 0]
        for och in range(16):
            par = och % 2
            p.wait("gpsimd", mm, wq_mark[par])
            src = wsrc_q[:, ts(och)].rearrange("(c p) o -> p c o", p=128)
            wn = gdma(wst[:, par * 2048:par * 2048 + 2048].rearrange("p (c o) -> p c o", o=128), src)
            qp = psA[:, 0:1024] if par == 0 else psA[:, 1024:2048]
            p.wait("tensor", gld, wn)
            for dc in range(NDC):
                wxo("tensor", dc)
            p.wait("tensor", ev, ev.n)
            p.wait("tensor", se, se.n)
            for half in range(2):
                for dc in range(NDC):
                    p.emit("tensor", lambda e, dc=dc, half=half, par=par, qp=qp:
                           e.matmul(qp[:, half * 512:(half + 1) * 512],
                                    wst[:, par * 2048 + dc * 128: par * 2048 + (dc + 1) * 128],
                                    (xf if TEST_XF else xofm)[dc][:, half * 512:(half + 1) * 512],
                                    start=(dc == 0), stop=(dc == NDC - 1)),
                           inc=(mm, 1) if dc == NDC - 1 else None)
            wq_mark[par] = mm.n
            mmq = mm.n
            if debug == 2 and not L1:
                p.wait("vector", mm, mmq)
                p.wait("vector", sld, sld.n)
                p.emit("vector", lambda e, qp=qp: e.tensor_copy(
                    out=x1t[:, 0:1024], in_=qp), inc=(ev, 1))
                p.wait("sync", ev, ev.n)
                sdma(dbg["dqp"][ts(och), :], x1t[:, 0:1024])
            if L1:
                p.wait("vector", mm, mmq)
                p.emit("vector", lambda e, och=och, qp=qp: e.tensor_scalar(
                    out=qf[och], in0=qp, scalar1=bq_s[:, och:och + 1], scalar2=1.0 / 128.0,
                    op0=ALU.add, op1=ALU.mult), inc=(ev, 1))
            else:
                p.wait("vector", mm, max(mmq, QUANT_MM))
                p.wait("vector", se, QUANT_SE)
                p.wait("vector", sld, QUANT_ST)
                p.wait("vector", gld, CONST_GLD)
                p.wait("vector", ev, ev.n)
                ra = rsc1
                rb = rsc2
                p.emit("vector", lambda e, qp=qp, ra=ra: e.tensor_mul(out=ra, in0=qp, in1=cosq_s), inc=(ev, 1))
                p.emit("vector", lambda e, qp=qp, rb=rb: e.tensor_mul(
                    out=rb[0:64, :], in0=qp[64:128, :], in1=sinq_s[0:64, :]), inc=(ev, 1))
                p.emit("vector", lambda e, qp=qp, rb=rb: e.tensor_mul(
                    out=rb[64:128, :], in0=qp[0:64, :], in1=sinq_s[64:128, :]), inc=(ev, 1))
                p.wait("vector", ev, ev.n)
                p.emit("vector", lambda e, och=och, ra=ra, rb=rb: e.tensor_sub(
                    out=qf[och][0:64, :], in0=ra[0:64, :], in1=rb[0:64, :]), inc=(ev, 1))
                p.emit("vector", lambda e, och=och, ra=ra, rb=rb: e.tensor_add(
                    out=qf[och][64:128, :], in0=ra[64:128, :], in1=rb[64:128, :]), inc=(ev, 1))

        wsrc_k = wkT if not L1 else wqkv[:, D:D + KVH * DH]
        wk_mark = [mm.n, mm.n]
        for och in range(KVH):
            par = och % 2
            p.wait("gpsimd", mm, wk_mark[par])
            src = wsrc_k[:, ts(och)].rearrange("(c p) o -> p c o", p=128)
            wn = gdma(wst[:, par * 2048:par * 2048 + 2048].rearrange("p (c o) -> p c o", o=128), src)
            p.wait("tensor", gld, wn)
            for dc in range(NDC):
                wxf("tensor", dc)
            p.wait("tensor", ev, max(ev.n, QUANT_EV))
            p.wait("tensor", se, se.n)
            for quad in range(4):
                for dc in range(NDC):
                    p.emit("tensor", lambda e, dc=dc, quad=quad, par=par:
                           e.matmul(psA[:, ts(quad, 512)],
                                    wst[:, par * 2048 + dc * 128: par * 2048 + (dc + 1) * 128],
                                    xf[dc][:, ts(quad, 512)],
                                    start=(dc == 0), stop=(dc == NDC - 1)),
                           inc=(mm, 1) if dc == NDC - 1 else None)
            wk_mark[par] = mm.n
            mmk = mm.n
            if L1:
                p.wait("vector", mm, mmk)
                p.emit("vector", lambda e, och=och: e.tensor_scalar_add(
                    out=kf[och], in0=psA, scalar1=bq_s[:, 16 + och:17 + och]), inc=(ev, 1))
            else:
                p.wait("vector", mm, mmk)
                p.wait("vector", gld, max(ckn, skn))
                for hh in range(2):
                    cs = coskb[:, hh * 1024:(hh + 1) * 1024]
                    sn = sinkb[:, hh * 1024:(hh + 1) * 1024]
                    pp = psA[:, hh * 1024:(hh + 1) * 1024]
                    dst = kf[och][:, hh * 1024:(hh + 1) * 1024]
                    p.wait("vector", ev, ev.n)
                    p.emit("vector", lambda e, pp=pp, cs=cs: e.tensor_mul(out=rsc1, in0=pp, in1=cs), inc=(ev, 1))
                    p.emit("vector", lambda e, pp=pp, sn=sn: e.tensor_mul(
                        out=rsc2[0:64, :], in0=pp[64:128, :], in1=sn[0:64, :]), inc=(ev, 1))
                    p.emit("vector", lambda e, pp=pp, sn=sn: e.tensor_mul(
                        out=rsc2[64:128, :], in0=pp[0:64, :], in1=sn[64:128, :]), inc=(ev, 1))
                    p.wait("vector", ev, ev.n)
                    p.emit("vector", lambda e, dst=dst: e.tensor_sub(
                        out=dst[0:64, :], in0=rsc1[0:64, :], in1=rsc2[0:64, :]), inc=(ev, 1))
                    p.emit("vector", lambda e, dst=dst: e.tensor_add(
                        out=dst[64:128, :], in0=rsc1[64:128, :], in1=rsc2[64:128, :]), inc=(ev, 1))

        wsrc_v = wvT if not L1 else wqkv[:, D + KVH * DH:]
        vmark = mm.n
        for vh in range(2):
            p.wait("gpsimd", mm, vmark)
            src = wsrc_v[:, vh * 256:(vh + 1) * 256].rearrange("(c p) o -> p c o", p=128)
            wn = gdma(wst.rearrange("p (c o) -> p c o", o=256), src)
            p.wait("tensor", gld, wn)
            p.wait("tensor", ev, ev.n)
            p.wait("tensor", se, max(se.n, QUANT_SE))
            for g in range(16):
                slot = g % 4
                vp = psA[:, slot * 512: slot * 512 + 256]
                p.wait("tensor", ev, ev.n)
                for dc in range(NDC):
                    wxf("tensor", dc)
                for dc in range(NDC):
                    p.emit("tensor", lambda e, dc=dc, g=g, vp=vp:
                           e.matmul(vp, xf[dc][:, ts(g)], wst[:, dc * 256:(dc + 1) * 256],
                                    start=(dc == 0), stop=(dc == NDC - 1 and not L1)),
                           inc=(mm, 1) if (dc == NDC - 1 and not L1) else None)
                if L1:
                    p.wait("tensor", ev, ONES_EV)
                    p.wait("tensor", gld, CONST_GLD)
                    p.emit("tensor", lambda e, vh=vh, vp=vp:
                           e.matmul(vp, ones_r[0:1, 0:128], vb_s[0:1, vh * 256:(vh + 1) * 256],
                                    start=False, stop=True), inc=(mm, 1))
                p.wait("vector", mm, mm.n)
                p.emit("vector", lambda e, g=g, vh=vh, vp=vp: e.tensor_copy(
                    out=vt[g][:, vh * 256:(vh + 1) * 256], in_=vp), inc=(ev, 1))
            vmark = mm.n

        if not L1:
            state["CC_W"] = emit_wq_gather()

        QP_MM = mm.n
        QP_SE = se.n
        p.wait("vector", se, QP_SE)

        nmax = sm_f[:, 32:33]
        sums = sm_f[:, 33:34]
        rs = sm_f[:, 34:35]
        last_exp_se = se.n
        psa_ev = ev.n
        for l in range(NL):
            nk = nk_of(l) if not L1 else 16
            nw = nk * 128
            mb0 = min(2 * l + 1, 14)
            for h in range(NH):
                g = kvgrp(h)
                p.wait("tensor", se, last_exp_se)
                p.wait("tensor", ev, psa_ev)
                for c0 in range(0, nk, 4):
                    cw = min(4, nk - c0) * 128
                    p.emit("tensor", lambda e, h=h, l=l, g=g, c0=c0, cw=cw:
                           e.matmul(psA[:, c0 * 128: c0 * 128 + cw], qf[h][:, ts(l)],
                                    kf[g][:, c0 * 128: c0 * 128 + cw], start=True, stop=True),
                           inc=(mm, 1))
                mms = mm.n
                p.wait("vector", mm, mms)
                if not L1:
                    p.emit("vector", lambda e, l=l, mb0=mb0: e.tensor_tensor(
                        out=psA[:, mb0 * 128: mb0 * 128 + 256],
                        in0=psA[:, mb0 * 128: mb0 * 128 + 256],
                        in1=mask_s[:, l * 256:(l + 1) * 256], op=ALU.add), inc=(ev, 1))
                    p.wait("vector", ev, ev.n)
                p.emit("vector", lambda e, nw=nw: e.reduce_max(
                    out=nmax, in_=psA[:, 0:nw], axis=AX.X, negate=True), inc=(ev, 1))
                psa_ev = ev.n
                p.wait("scalar", ev, ev.n)
                p.wait("scalar", mm, mm.n)
                p.emit("scalar", lambda e, nw=nw: e.activation(
                    out=Pt[:, 0:nw], in_=psA[:, 0:nw], func=AF.Exp,
                    bias=nmax, scale=1.0, accum_out=sums), inc=(se, 1))
                last_exp_se = se.n
                p.wait("vector", se, last_exp_se)
                p.emit("vector", lambda e: e.reciprocal(out=rs, in_=sums), inc=(ev, 1))
                p.wait("vector", ev, ev.n)
                p.emit("vector", lambda e, nw=nw: e.tensor_scalar_mul(
                    out=Pt[:, 0:nw], in0=Pt[:, 0:nw], scalar1=rs), inc=(ev, 1))
                evP = ev.n
                avp = psE[:, 1024:1536] if h % 2 == 0 else psE[:, 1536:2048]
                p.wait("tensor", ev, evP)
                for r0 in range(0, nk, 8):
                    rn = min(8, nk - r0)
                    p.wait("tensor", ev, ev.n)
                    for b in range(rn):
                        p.emit("tensor", lambda e, r0=r0, b=b: e.transpose(
                            psPT[:, ts(b)], Pt[:, ts(r0 + b)], id_b), inc=(mm, 1))
                    p.wait("vector", mm, mm.n)
                    p.emit("vector", lambda e, rn=rn: e.tensor_copy(
                        out=PTt[:, 0:rn * 128], in_=psPT[:, 0:rn * 128]), inc=(ev, 1))
                    p.wait("tensor", ev, ev.n)
                    for b in range(rn):
                        kb = r0 + b
                        p.emit("tensor", lambda e, kb=kb, b=b, g=g, avp=avp:
                               e.matmul(avp[:, 0:128], vt[kb][:, ts(g)], PTt[:, ts(b)],
                                        start=(kb == 0), stop=(kb == nk - 1)),
                               inc=(mm, 1) if kb == nk - 1 else None)
                p.wait("vector", mm, max(mm.n, QP_MM))
                p.emit("vector", lambda e, h=h, l=l, avp=avp: e.tensor_copy(
                    out=afull[h][:, ts(l)], in_=avp[:, 0:128]), inc=(ev, 1))
        ATT_SE = se.n
        ATT_MM = mm.n
        if debug == 2 and not L1:
            p.wait("sync", se, ATT_SE)
            p.wait("sync", ev, ev.n)
            for i_ in range(16):
                sdma(dbg["dqf"][ts(i_), :], qf[i_])
                sdma(dbg["daf"][ts(i_), :], afull[i_])
                sdma(dbg["dvt"][ts(i_), :], vt[i_])
            for i_ in range(4):
                sdma(dbg["dkf"][ts(i_), :], kf[i_])
            DUMP_ST = sld.n

        wsrc_o = woT if not L1 else wout
        res_src = xo if not L1 else sp2
        zi_n = []
        for tc in range(NL):
            p.wait("sync", mm, QP_MM)
            if L1:
                p.wait("sync", sld, state["SP2_ST"])
            zi_n.append(sdma(zall[tc], res_src[ts(tc), :]))
        if L1:
            bon = gdma(bout_row, boutr)
        wo_mark = 0
        for og in range(4):
            p.wait("gpsimd", mm, max(ATT_MM, wo_mark))
            if debug == 2 and not L1:
                p.wait("gpsimd", sld, DUMP_ST)
            src = wsrc_o[:, ts(og, 512)].rearrange("(c p) o -> c p o", p=128)
            won = 0
            for i in range(16):
                won = gdma(wo_og[i], src[i])
            p.wait("tensor", gld, won)
            p.wait("tensor", se, ATT_SE)
            for tc in range(NL):
                op_ps = psA[:, (tc % 4) * 512:(tc % 4) * 512 + 512]
                p.wait("tensor", ev, ev.n)
                for i in range(16):
                    p.emit("tensor", lambda e, i=i, tc=tc, op_ps=op_ps:
                           e.matmul(op_ps, afull[i][:, ts(tc)], wo_og[i],
                                    start=(i == 0), stop=(i == 15 and not L1)),
                           inc=(mm, 1) if (i == 15 and not L1) else None)
                if L1:
                    p.wait("tensor", ev, ONES_EV)
                    p.wait("tensor", gld, bon)
                    p.emit("tensor", lambda e, og=og, op_ps=op_ps:
                           e.matmul(op_ps, ones_r[0:1, 0:128], bout_row[0:1, ts(og, 512)],
                                    start=False, stop=True), inc=(mm, 1))
                p.wait("vector", mm, mm.n)
                p.wait("vector", sld, zi_n[tc])
                p.emit("vector", lambda e, tc=tc, og=og, op_ps=op_ps: e.tensor_add(
                    out=zall[tc][:, ts(og, 512)], in0=op_ps, in1=zall[tc][:, ts(og, 512)]),
                    inc=(ev, 1))
            wo_mark = mm.n

        LN_EV = ln_batch()
        st_bf = 0
        for tc in range(NL):
            p.wait("scalar", ev, LN_EV)
            p.wait("scalar", sld, sld.n)
            p.emit("scalar", lambda e, tc=tc: e.activation(
                out=zsq, in_=zall[tc], func=AF.Identity,
                bias=inv8[:, 8 + tc:9 + tc], scale=inv8[:, tc:tc + 1]), inc=(se, 1))
            p.wait("sync", se, se.n)
            st_bf = sdma(sb_bf[ts(tc), :], zsq)
            p.wait("vector", ev, LN_EV)
            p.wait("vector", se, se.n)
            p.emit("vector", lambda e, tc=tc: e.tensor_scalar(
                out=zall[tc], in0=zall[tc], scalar1=inv8[:, tc:tc + 1],
                scalar2=inv8[:, 8 + tc:9 + tc], op0=ALU.mult, op1=ALU.add), inc=(ev, 1))
            if layer == 0:
                p.wait("sync", ev, ev.n)
                sdma(sp1[ts(tc), :], zall[tc])
        state["Y_BF_ST"] = st_bf
        state["Y_EV"] = ev.n

    def emit_ffn():
        p.wait("scalar", mm, mm.n)
        p.wait("scalar", sld, state["Y_BF_ST"])
        x1_n = []
        for dc in range(NDC):
            for blk in range(2):
                src = sb_bf[blk * 512:(blk + 1) * 512, ts(dc)]
                tdma(x1fm[dc][:, blk * 512:(blk + 1) * 512], src, transpose=True)
            x1_n.append(tld.n)
        p.wait("sync", cc, state["CC_W"])

        p.emit("vector", lambda e: e.memset(ssc[:, 0:8], 0.0), inc=(ev, 1))
        p.wait("vector", ev, state["Y_EV"])
        p.wait("vector", mm, mm.n)
        p.wait("vector", se, se.n)
        SS_EV0 = ev.n

        g_mark = [mm.n, mm.n]
        last_h_ev = ev.n
        last_sq_se = se.n
        for j in range(64):
            par = j % 2
            rg, cg = j // 16, j % 16
            ru = 4 + j // 16
            gsrc = shT_gu[rg * 2048:(rg + 1) * 2048, ts(cg)].rearrange("(c p) h -> p c h", p=128)
            usrc = shT_gu[ru * 2048:(ru + 1) * 2048, ts(cg)].rearrange("(c p) h -> p c h", p=128)
            p.wait("sync", mm, g_mark[par])
            gdst = wst[:, par * 2048: par * 2048 + 2048]
            udst = upst0 if par == 0 else upst1
            gn = sdma(gdst.rearrange("p (c h) -> p c h", h=128), gsrc)
            un = sdma(udst.rearrange("p (c h) -> p c h", h=128), usrc)
            for dc in range(NDC):
                p.wait("tensor", tld, x1_n[dc])
            p.wait("tensor", sld, un)
            p.wait("tensor", ev, last_h_ev)
            p.wait("tensor", se, se.n)
            for half in range(2):
                for dc in range(NDC):
                    p.emit("tensor", lambda e, dc=dc, half=half, par=par:
                           e.matmul(psA[:, half * 512:(half + 1) * 512],
                                    wst[:, par * 2048 + dc * 128: par * 2048 + (dc + 1) * 128],
                                    x1fm[dc][:, half * 512:(half + 1) * 512],
                                    start=(dc == 0), stop=(dc == NDC - 1)),
                           inc=(mm, 1) if dc == NDC - 1 else None)
            for half in range(2):
                for dc in range(NDC):
                    p.emit("tensor", lambda e, dc=dc, half=half, par=par:
                           e.matmul(psA[:, 1024 + half * 512: 1024 + (half + 1) * 512],
                                    (upst0 if par == 0 else upst1)[:, dc * 128:(dc + 1) * 128],
                                    x1fm[dc][:, half * 512:(half + 1) * 512],
                                    start=(dc == 0), stop=(dc == NDC - 1)),
                           inc=(mm, 1) if dc == NDC - 1 else None)
            g_mark[par] = mm.n
            mmj = mm.n
            p.wait("scalar", mm, mmj)
            p.wait("scalar", ev, ev.n)
            p.emit("scalar", lambda e: e.activation(out=silu_b, in_=psA[:, 0:1024],
                                                    func=AF.Silu, bias=0.0, scale=1.0), inc=(se, 1))
            p.wait("vector", se, se.n)
            p.wait("vector", mm, mmj)
            p.emit("vector", lambda e, j=j: e.tensor_mul(out=hT[j], in0=psA[:, 1024:2048],
                                                         in1=silu_b), inc=(ev, 1))
            last_h_ev = ev.n
            p.wait("scalar", ev, last_h_ev)
            p.emit("scalar", lambda e, j=j: e.activation(out=zsq[:, 0:1024], in_=hT[j],
                                                         func=AF.Square, bias=0.0, scale=1.0), inc=(se, 1))
            last_sq_se = se.n
            p.wait("tensor", se, last_sq_se)
            p.wait("tensor", ev, ev.n)
            for tc in range(NL):
                p.emit("tensor", lambda e, tc=tc: e.matmul(
                    psE[:, tc:tc + 1], zsq[:, ts(tc)], onesc, start=True, stop=True),
                    inc=(mm, 1) if tc == NL - 1 else None)
            p.wait("vector", mm, mm.n)
            p.wait("vector", se, se.n)
            p.emit("vector", lambda e: e.tensor_add(out=ssc[:, 0:8], in0=ssc[:, 0:8],
                                                    in1=psE[:, 0:8]), inc=(ev, 1))
        p.wait("vector", ev, ev.n)
        p.emit("vector", lambda e: e.tensor_scalar(out=ssc[:, 8:16], in0=ssc[:, 0:8],
                                                   scalar1=1.0 / FH, scalar2=RMS_EPS,
                                                   op0=ALU.mult, op1=ALU.add), inc=(ev, 1))
        p.wait("scalar", ev, ev.n)
        p.emit("scalar", lambda e: e.activation(out=ssc[:, 8:16], in_=ssc[:, 8:16],
                                                func=AF.Ln, bias=0.0, scale=1.0), inc=(se, 1))
        p.wait("scalar", se, se.n)
        p.emit("scalar", lambda e: e.activation(out=ssc[:, 8:16], in_=ssc[:, 8:16],
                                                func=AF.Exp, bias=0.0, scale=-0.5), inc=(se, 1))
        RMS_SE = se.n

        w_mark = [mm.n, mm.n]
        y2_st = 0
        for pair in range(4):
            tca, tcb = 2 * pair, 2 * pair + 1
            p.wait("tensor", ev, ev.n)
            for hc in range(64):
                par = hc % 2
                p.wait("sync", mm, w_mark[par])
                src = shT_wp.rearrange("(r q p) o -> q p r o", q=64, p=128)[hc]
                dst = wst[:, par * 2048: par * 2048 + 2048].rearrange("p (r o) -> p r o", o=256)
                wn = sdma(dst, src)
                p.wait("tensor", sld, wn)
                for t_i, tc in ((0, tca), (1, tcb)):
                    pps = psA if t_i == 0 else psE
                    for og8 in range(4):
                        p.emit("tensor", lambda e, hc=hc, tc=tc, og8=og8, pps=pps, par=par:
                               e.matmul(pps[:, og8 * 512:(og8 + 1) * 512],
                                        hT[hc][:, ts(tc)],
                                        wst[:, par * 2048 + og8 * 512: par * 2048 + (og8 + 1) * 512],
                                        start=(hc == 0), stop=(hc == 63)),
                               inc=(mm, 1) if (og8 == 3 and t_i == 1) else None)
                w_mark[par] = mm.n
            DN_MM = mm.n
            for t_i, tc in ((0, tca), (1, tcb)):
                pps = psA if t_i == 0 else psE
                p.wait("sync", ev, ev.n)
                rn = sdma(x1t, sp1[ts(tc), :])
                p.wait("vector", mm, DN_MM)
                p.wait("vector", sld, rn)
                p.wait("vector", se, RMS_SE)
                p.emit("vector", lambda e, pps=pps, tc=tc: e.scalar_tensor_tensor(
                    out=x1t, in0=pps, scalar=ssc[:, 8 + tc:9 + tc], in1=x1t,
                    op0=ALU.mult, op1=ALU.add, accum_out=sm_f[:, 0:1]), inc=(ev, 1))
                zev = ev.n
                p.wait("scalar", ev, zev)
                p.wait("scalar", se, se.n)
                p.emit("scalar", lambda e: e.activation(
                    out=zsq, in_=x1t, func=AF.Square, bias=0.0, scale=1.0,
                    accum_out=sm_f[:, 1:2]), inc=(se, 1))
                p.wait("vector", se, se.n)
                p.emit("vector", lambda e: e.tensor_scalar_mul(out=sm_f[:, 2:3], in0=sm_f[:, 0:1],
                                                               scalar1=1.0 / D), inc=(ev, 1))
                p.wait("vector", ev, ev.n)
                p.emit("vector", lambda e: e.tensor_mul(out=sm_f[:, 3:4], in0=sm_f[:, 2:3],
                                                        in1=sm_f[:, 2:3]), inc=(ev, 1))
                p.wait("vector", ev, ev.n)
                p.emit("vector", lambda e: e.tensor_scalar(out=sm_f[:, 4:5], in0=sm_f[:, 1:2],
                                                           scalar1=1.0 / D, scalar2=None,
                                                           op0=ALU.mult), inc=(ev, 1))
                p.wait("vector", ev, ev.n)
                p.emit("vector", lambda e: e.tensor_sub(out=sm_f[:, 4:5], in0=sm_f[:, 4:5],
                                                        in1=sm_f[:, 3:4]), inc=(ev, 1))
                p.wait("vector", ev, ev.n)
                p.emit("vector", lambda e: e.tensor_scalar_add(out=sm_f[:, 4:5], in0=sm_f[:, 4:5],
                                                               scalar1=LN_EPS), inc=(ev, 1))
                p.wait("scalar", ev, ev.n)
                p.emit("scalar", lambda e: e.activation(out=sm_f[:, 5:6], in_=sm_f[:, 4:5],
                                                        func=AF.Ln, bias=0.0, scale=1.0), inc=(se, 1))
                p.wait("scalar", se, se.n)
                p.emit("scalar", lambda e: e.activation(out=sm_f[:, 6:7], in_=sm_f[:, 5:6],
                                                        func=AF.Exp, bias=0.0, scale=-0.5), inc=(se, 1))
                p.wait("vector", se, se.n)
                p.emit("vector", lambda e: e.tensor_mul(out=sm_f[:, 7:8], in0=sm_f[:, 2:3],
                                                        in1=sm_f[:, 6:7]), inc=(ev, 1))
                p.wait("vector", ev, ev.n)
                p.emit("vector", lambda e: e.tensor_scalar_mul(out=sm_f[:, 7:8], in0=sm_f[:, 7:8],
                                                               scalar1=-1.0), inc=(ev, 1))
                p.wait("scalar", ev, ev.n)
                p.emit("scalar", lambda e: e.activation(
                    out=zsq, in_=x1t, func=AF.Identity,
                    bias=sm_f[:, 7:8], scale=sm_f[:, 6:7]), inc=(se, 1))
                p.wait("sync", se, se.n)
                sdma(cc_x_in[ts(tc), :], zsq)
                p.wait("vector", se, se.n)
                p.emit("vector", lambda e: e.tensor_scalar(
                    out=x1t, in0=x1t, scalar1=sm_f[:, 6:7],
                    scalar2=sm_f[:, 7:8], op0=ALU.mult, op1=ALU.add), inc=(ev, 1))
                p.wait("sync", ev, ev.n)
                y2_st = sdma(sp2[ts(tc), :], x1t)
        state["SP2_ST"] = y2_st
        p.wait("gpsimd", sld, sld.n)
        p.emit("gpsimd", lambda e: e.collective_compute(
            "AllGather", ALU.bypass, [[0, 1], [2, 3], [4, 5], [6, 7]],
            [cc_x_in.opt()], [cc_x_out.opt()]), inc=(cc, 1))
        p.wait("gpsimd", cc, cc.n)
        p.wait("sync", cc, cc.n)
        p.wait("scalar", cc, cc.n)
        state["CC_X"] = cc.n

    def emit_moe():
        p.wait("scalar", mm, mm.n)
        p.wait("scalar", sld, state["Y_BF_ST"])
        x3_n = []
        for dc in range(NDC):
            for blk in range(2):
                src = sb_bf[blk * 512:(blk + 1) * 512, ts(dc)]
                tdma(x3fm[dc][:, blk * 512:(blk + 1) * 512], src, transpose=True)
            x3_n.append(tld.n)
        p.wait("vector", sld, sld.n)
        p.wait("gpsimd", ev, ev.n)
        p.wait("gpsimd", se, se.n)
        p.wait("gpsimd", sld, sld.n)
        p.wait("gpsimd", tld, tld.n)

        p.wait("tensor", gld, CONST_GLD)
        for tc in range(NL):
            for dc in range(NDC):
                p.wait("tensor", tld, x3_n[dc])
            p.wait("tensor", ev, ev.n)
            for dc in range(NDC):
                p.emit("tensor", lambda e, tc=tc, dc=dc: e.matmul(
                    psE[:, 0:32], x3fm[dc][:, ts(tc)], gw_s[:, dc * E:(dc + 1) * E],
                    start=(dc == 0), stop=(dc == NDC - 1)),
                    inc=(mm, 1) if dc == NDC - 1 else None)
            mml = mm.n
            lo, lt, eq = t8[:, 0:32], t8[:, 32:64], t8[:, 64:96]
            cur, nmx, tau, den = sm_f[:, 0:1], sm_f[:, 1:2], sm_f[:, 2:3], sm_f[:, 3:4]
            p.wait("vector", mm, mml)
            p.emit("vector", lambda e: e.tensor_copy(out=lo, in_=psE[:, 0:32]), inc=(ev, 1))
            p.wait("vector", ev, ev.n)
            p.emit("vector", lambda e: e.tensor_copy(out=lt, in_=lo), inc=(ev, 1))
            p.wait("vector", ev, ev.n)
            for it in range(TOPK):
                p.emit("vector", lambda e: e.reduce_max(out=cur, in_=lt, axis=AX.X), inc=(ev, 1))
                p.wait("vector", ev, ev.n)
                if it == 0:
                    p.emit("vector", lambda e: e.tensor_scalar_mul(out=nmx, in0=cur,
                                                                   scalar1=-1.0), inc=(ev, 1))
                    p.wait("vector", ev, ev.n)
                if it == TOPK - 1:
                    p.emit("vector", lambda e: e.tensor_copy(out=tau, in_=cur), inc=(ev, 1))
                    p.wait("vector", ev, ev.n)
                    break
                p.emit("vector", lambda e: e.tensor_scalar(out=eq, in0=lt, scalar1=cur,
                                                           scalar2=-BIG, op0=ALU.is_equal,
                                                           op1=ALU.mult), inc=(ev, 1))
                p.wait("vector", ev, ev.n)
                p.emit("vector", lambda e: e.tensor_add(out=lt, in0=lt, in1=eq), inc=(ev, 1))
                p.wait("vector", ev, ev.n)
            p.wait("scalar", ev, ev.n)
            p.wait("scalar", se, se.n)
            p.emit("scalar", lambda e: e.activation(out=lt, in_=lo, func=AF.Exp,
                                                    bias=nmx, scale=1.0), inc=(se, 1))
            p.wait("vector", se, se.n)
            p.emit("vector", lambda e: e.tensor_scalar(out=eq, in0=lo, scalar1=tau,
                                                       scalar2=None, op0=ALU.is_ge), inc=(ev, 1))
            p.wait("vector", ev, ev.n)
            p.emit("vector", lambda e: e.tensor_mul(out=lt, in0=lt, in1=eq), inc=(ev, 1))
            p.wait("vector", ev, ev.n)
            p.emit("vector", lambda e: e.reduce_sum(out=den, in_=lt, axis=AX.X), inc=(ev, 1))
            p.wait("vector", ev, ev.n)
            p.emit("vector", lambda e: e.reciprocal(out=den, in_=den), inc=(ev, 1))
            p.wait("vector", ev, ev.n)
            p.emit("vector", lambda e, tc=tc: e.tensor_scalar_mul(
                out=mw_s[:, tc * E:(tc + 1) * E], in0=lt, scalar1=den), inc=(ev, 1))
            p.wait("vector", ev, ev.n)

        if debug == 2:
            p.wait("sync", ev, ev.n)
            sdma(dbg["dmw"], mw_s)

        nslab = 6 * E
        slab_mark = [mm.n, mm.n, mm.n]

        def slab_load(e_i, ic):
            si = (e_i * 6 + ic) % 3
            p.wait("gpsimd", mm, slab_mark[si])
            gn = gdma(wgS[si].rearrange("p (c h) -> p c h", h=128),
                      mgT[e_i, :, ts(ic)].rearrange("(c p) i -> p c i", p=128))
            un = gdma(wuS[si].rearrange("p (c h) -> p c h", h=128),
                      muT[e_i, :, ts(ic)].rearrange("(c p) i -> p c i", p=128))
            return max(gn, un)

        slab_n = {}
        for ic in range(3):
            slab_n[(0, ic)] = slab_load(0, ic)
        wd_mark = [mm.n, mm.n]
        dn_last = {}
        gsl_se = [se.n, se.n]
        gsl_ev = [ev.n, ev.n]
        msl_ev = [ev.n, ev.n]
        dsl_ev = [ev.n] * 4
        for e_i in range(E):
            for ic in range(6):
                si = (e_i * 6 + ic) % 3
                p.wait("tensor", gld, slab_n[(e_i, ic)])
                for dc in range(NDC):
                    p.wait("tensor", tld, x3_n[dc])
                for tg in range(2):
                    slot = tg
                    gp = psA[:, slot * 512: slot * 512 + 512]
                    up = psA[:, 1024 + slot * 512: 1024 + slot * 512 + 512]
                    p.wait("tensor", se, gsl_se[slot])
                    p.wait("tensor", ev, gsl_ev[slot])
                    for dc in range(NDC):
                        p.emit("tensor", lambda e, dc=dc, si=si, tg=tg, gp=gp:
                               e.matmul(gp, wgS[si][:, dc * 128:(dc + 1) * 128],
                                        x3fm[dc][:, tg * 512:(tg + 1) * 512],
                                        start=(dc == 0), stop=(dc == NDC - 1)),
                               inc=(mm, 1) if dc == NDC - 1 else None)
                    for dc in range(NDC):
                        p.emit("tensor", lambda e, dc=dc, si=si, tg=tg, up=up:
                               e.matmul(up, wuS[si][:, dc * 128:(dc + 1) * 128],
                                        x3fm[dc][:, tg * 512:(tg + 1) * 512],
                                        start=(dc == 0), stop=(dc == NDC - 1)),
                               inc=(mm, 1) if dc == NDC - 1 else None)
                    mmg = mm.n
                    sis = msilu[tg]
                    p.wait("scalar", mm, mmg)
                    p.wait("scalar", ev, msl_ev[tg])
                    p.emit("scalar", lambda e, gp=gp, sis=sis: e.activation(
                        out=sis, in_=gp, func=AF.Silu, bias=0.0, scale=1.0), inc=(se, 1))
                    gsl_se[slot] = se.n
                    p.wait("vector", se, se.n)
                    p.wait("vector", mm, dn_last.get(e_i - 1, 0))
                    if debug == 2 and e_i == 1:
                        p.wait("vector", sld, sld.n)
                    p.emit("vector", lambda e, ic=ic, tg=tg, up=up, sis=sis: e.tensor_mul(
                        out=hE[ic][:, tg * 512:(tg + 1) * 512], in0=up, in1=sis), inc=(ev, 1))
                    gsl_ev[slot] = ev.n
                    msl_ev[tg] = ev.n
                slab_mark[si] = mm.n
                if e_i * 6 + ic + 3 < nslab:
                    nxt = e_i * 6 + ic + 3
                    slab_n[(nxt // 6, nxt % 6)] = slab_load(nxt // 6, nxt % 6)
            H_EV = ev.n
            if debug == 2 and e_i == 0:
                p.wait("sync", ev, H_EV)
                for ic_ in range(6):
                    sdma(dbg["dhe"][ts(ic_), :], hE[ic_])
                DHE_ST = sld.n
            if debug == 2 and e_i == 1:
                pass
            for og in range(4):
                wpar = og % 2
                p.wait("gpsimd", mm, wd_mark[wpar])
                wn = gdma(wdR[wpar].rearrange("p (c o) -> p c o", o=512),
                          mdT[e_i].rearrange("(c p) o -> p c o", p=128)[:, :, ts(og, 512)])
                p.wait("tensor", gld, wn)
                p.wait("tensor", ev, H_EV)
                for tc in range(NL):
                    slot = tc % 4
                    dp = psE[:, slot * 512: slot * 512 + 512]
                    p.wait("tensor", ev, dsl_ev[slot])
                    for ic in range(6):
                        p.emit("tensor", lambda e, ic=ic, tc=tc, wpar=wpar, dp=dp:
                               e.matmul(dp, hE[ic][:, ts(tc)],
                                        wdR[wpar][:, ic * 512:(ic + 1) * 512],
                                        start=(ic == 0), stop=(ic == 5)),
                               inc=(mm, 1) if ic == 5 else None)
                    p.wait("vector", mm, mm.n)
                    p.emit("vector", lambda e, tc=tc, og=og, e_i=e_i, dp=dp: e.scalar_tensor_tensor(
                        out=zall[tc][:, ts(og, 512)], in0=dp,
                        scalar=mw_s[:, tc * E + e_i: tc * E + e_i + 1],
                        in1=zall[tc][:, ts(og, 512)], op0=ALU.mult, op1=ALU.add), inc=(ev, 1))
                    dsl_ev[slot] = ev.n
                wd_mark[wpar] = mm.n
            dn_last[e_i] = mm.n

        LN_EV = ln_batch()
        for tc in range(NL):
            p.wait("vector", ev, LN_EV)
            p.emit("vector", lambda e, tc=tc: e.tensor_scalar(
                out=zall[tc], in0=zall[tc], scalar1=inv8[:, tc:tc + 1],
                scalar2=inv8[:, 8 + tc:9 + tc], op0=ALU.mult, op1=ALU.add), inc=(ev, 1))
            p.wait("sync", ev, ev.n)
            sdma(y_out[ts(tc), :], zall[tc])

    emit_attention(0)
    if debug:
        p.wait("gpsimd", sld, sld.n)
        gdma(dbg["dx1"], sp1)
    emit_ffn()
    if debug:
        p.wait("gpsimd", sld, sld.n)
        gdma(dbg["dx2"], sp2)
    emit_attention(1)
    if debug:
        p.wait("sync", ev, ev.n)
        for tc in range(NL):
            sdma(dbg["dx3"][ts(tc), :], zall[tc])
    emit_moe()

    p.wait("sync", sld, sld.n)
    p.wait("sync", gld, gld.n)
    p.wait("sync", tld, tld.n)
    return p.finalize()


def _rope_tables():
    inv = (1.0 / THETA ** (np.arange(0, DH, 2, dtype=np.float32) / np.float32(DH))).astype(np.float32)
    pos = np.arange(S, dtype=np.float32)
    fr = pos[:, None] * inv[None, :]
    cos = np.repeat(np.cos(fr), 2, axis=-1).astype(np.float32)
    sin = np.repeat(np.sin(fr), 2, axis=-1).astype(np.float32)
    return cos, sin


def _mask_tiles(m):
    tri = np.triu(np.full((128, 128), NEG, dtype=np.float32), 1)
    zero = np.zeros((128, 128), dtype=np.float32)
    neg = np.full((128, 128), NEG, dtype=np.float32)
    out = np.zeros((NL, 128, 256), dtype=np.float32)
    for l in range(NL):
        G = 2 * l + m
        mb0 = min(2 * l + 1, 14)
        for w in range(2):
            b = mb0 + w
            t = zero if b <= G else (tri if b == G + 1 else neg)
            out[l, :, w * 128:(w + 1) * 128] = t
    return out


_CACHE = {}


def kernel(**inputs):
    if "nc" not in _CACHE:
        _CACHE["nc"] = build(debug=False)
    nc = _CACHE["nc"]

    x = np.ascontiguousarray(np.asarray(inputs["L0_attn_x"], dtype=np.float32))
    gu = np.ascontiguousarray(np.asarray(inputs["L0_ffn_w_gate_up"], dtype=np.float32))
    wp = np.ascontiguousarray(np.asarray(inputs["L0_ffn_w_proj"], dtype=np.float32))
    wqkv = np.ascontiguousarray(np.asarray(inputs["L1_attn_Wqkv"], dtype=np.float32))
    bqkv = np.asarray(inputs["L1_attn_bqkv"], dtype=np.float32)
    wout = np.ascontiguousarray(np.asarray(inputs["L1_attn_Wout"], dtype=np.float32))
    bout = np.asarray(inputs["L1_attn_bout"], dtype=np.float32)

    wqT = np.ascontiguousarray(np.asarray(inputs["L0_attn_Wq"], dtype=np.float32).T)
    wkT = np.ascontiguousarray(np.asarray(inputs["L0_attn_Wk"], dtype=np.float32).T)
    wvT = np.ascontiguousarray(np.asarray(inputs["L0_attn_Wv"], dtype=np.float32).T)
    woT = np.ascontiguousarray(np.asarray(inputs["L0_attn_Wo"], dtype=np.float32).T)
    gwT = np.ascontiguousarray(np.asarray(inputs["L1_ffn_gate_w"], dtype=np.float32).T)
    mgT = np.ascontiguousarray(np.asarray(inputs["L1_ffn_w_gate_proj"], dtype=np.float32).transpose(0, 2, 1))
    muT = np.ascontiguousarray(np.asarray(inputs["L1_ffn_w_up_proj"], dtype=np.float32).transpose(0, 2, 1))
    mdT = np.ascontiguousarray(np.asarray(inputs["L1_ffn_w_down_proj"], dtype=np.float32).transpose(0, 2, 1))
    bqkv_c = np.ascontiguousarray(bqkv.reshape(24, 128).T)
    vb = np.ascontiguousarray(bqkv[None, D + KVH * DH:])
    boutr = np.ascontiguousarray(bout[None, :])
    cos, sin = _rope_tables()
    coskT = np.ascontiguousarray(cos.T)
    sinkT = np.ascontiguousarray(sin.T)
    ident = np.eye(128, dtype=np.float32)

    in_maps = []
    for c in range(8):
        s, m = c // 2, c % 2
        own = np.concatenate([np.arange((2 * l + m) * 128, (2 * l + m + 1) * 128)
                              for l in range(NL)])
        xs = x[s]
        xsT = np.ascontiguousarray(xs.T)
        in_maps.append({
            "xo": np.ascontiguousarray(xs[own, :]),
            "xoT": np.ascontiguousarray(xsT[:, own]),
            "xaT": xsT,
            "wqT": wqT, "wkT": wkT, "wvT": wvT, "woT": woT,
            "gu_sh": np.ascontiguousarray(gu[c * (2 * FH // 8):(c + 1) * (2 * FH // 8), :]),
            "wp_sh": np.ascontiguousarray(wp[c * (D // 8):(c + 1) * (D // 8), :]),
            "wqkv": wqkv, "bqkv_c": bqkv_c, "vb": vb,
            "wout": wout, "boutr": boutr,
            "gwT": gwT, "mgT": mgT, "muT": muT, "mdT": mdT,
            "cosq": np.ascontiguousarray(cos[own, :].T * np.float32(DHS)),
            "sinq": np.ascontiguousarray(sin[own, :].T * np.float32(DHS)),
            "cosk": coskT, "sink": sinkT,
            "mask0": _mask_tiles(m), "ident": ident,
        })

    res = run_bass_kernel_spmd(nc, in_maps, core_ids=list(range(8)))
    out = np.empty((B, S, D), dtype=np.float32)
    for c in range(8):
        s, m = c // 2, c % 2
        y = np.asarray(res.results[c]["y"])
        for l in range(NL):
            out[s, (2 * l + m) * 128:(2 * l + m + 1) * 128, :] = y[ts(l), :]
    return out
